# version 18
# baseline (speedup 1.0000x reference)
import contextlib
import numpy as np

import concourse.bass as bass
import concourse.mybir as mybir
from concourse.bass_utils import run_bass_kernel_spmd

F32 = mybir.dt.float32
BF16 = mybir.dt.bfloat16
AF = mybir.ActivationFunctionType
ALU = mybir.AluOpType
AX = mybir.AxisListType.X

D = 2048
S = 4096
NB = 4
OWN = 2048
HALF = 1024
NKV = 8
NQ = 32
NG = 4
NE = 8
NH = 1408
EPS = 1e-5
SLOPES = [2.0 ** (-g / NG) for g in range(NG)]

OWN_SIGMAS = list(range(-640, 257, 128))
OTHER_SIGMAS = [-640, -512, -3968, -3840]
NEBL = len(OWN_SIGMAS) + len(OTHER_SIGMAS)


def kb_entries(qcbase):
    out = []
    for kb in range(32):
        sig = qcbase - kb * 128
        if kb < 16 and sig in OWN_SIGMAS:
            out.append((kb, OWN_SIGMAS.index(sig)))
        elif kb >= 16 and sig in OTHER_SIGMAS:
            out.append((kb, len(OWN_SIGMAS) + OTHER_SIGMAS.index(sig)))
    return out


def q_tile_of_head(h):
    kv, g = h // NG, h % NG
    return (kv // 2) * NG + g, 64 * (kv % 2)


class Cnt:
    def __init__(self, es, nc, name):
        self.h = es.enter_context(nc.semaphore(name))
        self.v = 0

    def bump(self, n=1):
        self.v += n
        return self.v


class Plan:
    ENGS = ("sync", "tensor", "scalar", "vector", "gpsimd")

    def __init__(self):
        self.streams = {e: [] for e in self.ENGS}
        self.seen = {e: {} for e in self.ENGS}

    def wait(self, eng, cnt, v=None):
        val = cnt.v if v is None else v
        if val <= 0:
            return
        if self.seen[eng].get(id(cnt), 0) >= val:
            return
        self.seen[eng][id(cnt)] = val
        self.streams[eng].append(lambda e, h=cnt.h, val=val: e.wait_ge(h, val))

    def op(self, eng, fn, inc=None, n=1):
        if inc is None:
            self.streams[eng].append(fn)
            return None
        v = inc.bump(n)
        self.streams[eng].append(lambda e, fn=fn, h=inc.h, n=n: fn(e).then_inc(h, n))
        return v


def build_program(dbg=False):
    nc = bass.Bass("TRN2", target_bir_lowering=False)

    X = nc.dram_tensor("X", [S, D], F32, kind="ExternalInput").ap()
    WQ = nc.dram_tensor("WQ", [16, 128, 16, 128], F32, kind="ExternalInput").ap()
    WK = nc.dram_tensor("WK", [4, 128, 16, 128], F32, kind="ExternalInput").ap()
    WV = nc.dram_tensor("WV", [4, 128, 16, 128], F32, kind="ExternalInput").ap()
    WO = nc.dram_tensor("WO", [16, 128, D], F32, kind="ExternalInput").ap()
    BQ = nc.dram_tensor("BQ", [16, 128], F32, kind="ExternalInput").ap()
    BK = nc.dram_tensor("BK", [4, 128], F32, kind="ExternalInput").ap()
    BV = nc.dram_tensor("BV", [4, 128], F32, kind="ExternalInput").ap()
    GUP = nc.dram_tensor("GUP", [NE, 11, 128, 16, 256], F32, kind="ExternalInput").ap()
    DWN = nc.dram_tensor("DWN", [NE, 4, 128, 11, 512], F32, kind="ExternalInput").ap()
    RTR = nc.dram_tensor("RTR", [128, 16, 8], F32, kind="ExternalInput").ap()
    EYE = nc.dram_tensor("EYE", [128, 128], F32, kind="ExternalInput").ap()
    EBL = nc.dram_tensor("EBL", [NG, 128, NEBL, 512], F32, kind="ExternalInput").ap()

    OUT = nc.dram_tensor("OUT", [OWN, D], F32, kind="ExternalOutput").ap()

    spk = "ExternalOutput" if dbg else "Internal"
    SPXNT = nc.dram_tensor("SPXNT", [2, 128, 16, HALF], BF16, kind=spk).ap()
    SPAOT = nc.dram_tensor("SPAOT", [4, 128, 16, 512], BF16, kind=spk).ap()
    SPS = nc.dram_tensor("SPS", [16, 128, D], BF16, kind=spk).ap()
    if dbg:
        DKT = nc.dram_tensor("DKT", [128, 4, S], BF16, kind="ExternalOutput").ap()
        DVA = nc.dram_tensor("DVA", [128, 32, 8, 65], BF16, kind="ExternalOutput").ap()
        DQT = nc.dram_tensor("DQT", [4, 128, 16, 512], BF16, kind="ExternalOutput").ap()
        DNC = nc.dram_tensor("DNC", [4, 128, 32], F32, kind="ExternalOutput").ap()
        DCB = nc.dram_tensor("DCB", [2, 128, 8, 8], F32, kind="ExternalOutput").ap()

    P = Plan()
    es = contextlib.ExitStack()
    free = {}

    def free_set(key, *events):
        free[key] = list(events)

    def wait_free(eng, key):
        for (cnt, v) in free.get(key, ()):
            P.wait(eng, cnt, v)

    with es:
        sbuf = lambda name, shape, dt: es.enter_context(nc.sbuf_tensor(name, shape, dt))

        eye = sbuf("eye", [128, 128], BF16)
        eyef = sbuf("eyef", [128, 128], F32)
        onesf = sbuf("onesf", [128, 128], F32)
        ones_b16 = sbuf("ones_b16", [128, 64], BF16)
        bq8 = sbuf("bq8", [128, 16], F32)
        bk_sb = sbuf("bk_sb", [128, 4], F32)
        bv_sb = sbuf("bv_sb", [128, 4], F32)
        negc = sbuf("negc", [128, 32], F32)
        kmax2 = sbuf("kmax2", [1, 64], F32)
        qmax2 = sbuf("qmax2", [1, 64], F32)
        tiny = sbuf("tiny", [1, 64], F32)
        xnT_h = sbuf("xnT_h", [128, 16, HALF], BF16)
        es2 = contextlib.ExitStack()
        sbuf2 = lambda name, shape, dt: es2.enter_context(nc.sbuf_tensor(name, shape, dt))
        kT = sbuf2("kT", [128, 4, S], BF16)
        vaug = sbuf2("vaug", [128, 32, 8, 65], BF16)
        wbuf = sbuf2("wbuf", [128, 2, 16, 128], BF16)
        sqb = sbuf2("sqb", [128, 2, 512], BF16)

        sCst = Cnt(es, nc, "sCst")
        sCstG = Cnt(es, nc, "sCstG")
        sXB = [Cnt(es, nc, f"sXB{i}") for i in range(2)]
        sXB2 = [Cnt(es, nc, f"sXB2{i}") for i in range(2)]
        sXB3 = [Cnt(es, nc, f"sXB3{i}") for i in range(2)]
        sXB4 = [Cnt(es, nc, f"sXB4{i}") for i in range(2)]
        sWB = [Cnt(es, nc, f"sWB{i}") for i in range(2)]
        sEB = [Cnt(es, nc, f"sEB{i}") for i in range(2)]
        sAM = [Cnt(es, nc, f"sAM{i}") for i in range(2)]
        sGW = [Cnt(es, nc, f"sGW{i}") for i in range(2)]
        sDW = [Cnt(es, nc, f"sDW{i}") for i in range(2)]
        sSB = [Cnt(es, nc, f"sSB{i}") for i in range(2)]
        sXL = [Cnt(es, nc, f"sXL{i}") for i in range(2)]
        sO = [Cnt(es, nc, f"sO{i}") for i in range(2)]
        sXH = Cnt(es, nc, "sXH")
        sSm = [Cnt(es, nc, f"sSm{i}") for i in range(2)]
        sSp = Cnt(es, nc, "sSp")
        sT = Cnt(es, nc, "sT")
        sT2 = Cnt(es, nc, "sT2")
        sS = Cnt(es, nc, "sS")
        sS2 = Cnt(es, nc, "sS2")
        sV = Cnt(es, nc, "sV")
        sV2 = Cnt(es, nc, "sV2")
        sG = Cnt(es, nc, "sG")
        sOut = Cnt(es, nc, "sOut")
        sBar = Cnt(es, nc, "sBar")

        def barrier():
            base = sBar.v
            for eng in Plan.ENGS:
                P.op(eng, lambda e: e.nop(), inc=sBar)
            for eng in Plan.ENGS:
                P.wait(eng, sBar, base + 5)
            free.clear()

        P.op("gpsimd", lambda e: e.dma_start(out=eye[:], in_=EYE), inc=sCstG, n=16)
        P.op("sync", lambda e: e.dma_start(out=eyef[:], in_=EYE), inc=sCst, n=16)
        for t in range(16):
            P.op("sync", lambda e, t=t: e.dma_start(
                out=bq8[:, t:t + 1], in_=BQ[t:t + 1, :].rearrange("o p -> p o")), inc=sCst, n=16)
        for t in range(4):
            P.op("sync", lambda e, t=t: e.dma_start(
                out=bk_sb[:, t:t + 1], in_=BK[t:t + 1, :].rearrange("o p -> p o")), inc=sCst, n=16)
            P.op("sync", lambda e, t=t: e.dma_start(
                out=bv_sb[:, t:t + 1], in_=BV[t:t + 1, :].rearrange("o p -> p o")), inc=sCst, n=16)
        P.wait("vector", sCst)
        P.wait("vector", sCstG)
        P.op("vector", lambda e: e.memset(onesf[:], 1.0), inc=sV)
        P.op("vector", lambda e: e.memset(ones_b16[:], 1.0), inc=sV)
        P.op("vector", lambda e: e.memset(vaug[:, :, :, 64:65], 1.0), inc=sV)
        P.op("vector", lambda e: e.memset(kmax2[:], 0.0), inc=sV)
        P.op("vector", lambda e: e.tensor_scalar(bq8[:], bq8[:], 0.125, None, ALU.mult), inc=sV)
        bq8_ready = sV.v

        es1 = contextlib.ExitStack()
        sbuf1 = lambda name, shape, dt: es1.enter_context(nc.sbuf_tensor(name, shape, dt))
        xbuf = sbuf1("xbuf", [128, 7, D], F32)
        xnbuf = sbuf1("xnbuf", [128, 2, D], F32)
        st6 = sbuf1("st6", [128, 4, 4, 6], F32)
        stat = sbuf1("stat", [128, 4, 8], F32)
        xnT_t = sbuf1("xnT_t", [128, 16, 512], BF16)
        vtmp = sbuf1("vtmp", [128, 2, 512], F32)

        pp = [es.enter_context(nc.psum_tensor(f"pp{i}", [128, 2, 512], F32)) for i in range(3)]
        pp3 = es.enter_context(nc.psum_tensor("pp3", [128, 512], F32))
        pp4 = es.enter_context(nc.psum_tensor("pp4", [128, 512], F32))

        def bank(i):
            if i < 6:
                return pp[i // 2][:, i % 2, :]
            return pp3[:] if i == 6 else pp4[:]

        xq_v = {}
        S7 = sXB + sXB2 + sXB3 + sXB4

        def quad_dma(grp):
            rows0 = grp * 512
            for ti in range(4):
                buf = (4 * grp + ti) % 7
                q = "sync" if ti < 2 else "scalar"
                wait_free(q, ("xbuf", buf))
                v = P.op(q, lambda e, r0=rows0 + ti * 128, buf=buf: e.dma_start(
                    out=xbuf[:, buf, :], in_=X[r0:r0 + 128, :]), inc=S7[buf], n=16)
                xq_v[(grp, ti)] = (S7[buf], v)

        def ln_quad(grp, dst, dst_off):
            wait_free("vector", ("stat",))
            for ti in range(4):
                buf = (4 * grp + ti) % 7
                P.wait("vector", *xq_v.pop((grp, ti)))
                for i in range(4):
                    P.op("vector", lambda e, ti=ti, i=i, buf=buf: e.bn_stats(
                        st6[:, ti, i, :], xbuf[:, buf, i * 512:(i + 1) * 512]),
                        inc=(sV if i == 3 else None))
            P.wait("vector", sV)
            for ti in range(4):
                P.op("vector", lambda e, ti=ti: e.bn_aggr(
                    stat[:, ti, 0:2], st6[:, ti].rearrange("p a b -> p (a b)")),
                    inc=(sV if ti == 3 else None))
            P.wait("vector", sV)
            sv = lambda c: stat[:, :, c:c + 1].rearrange("p a o -> p (a o)")
            P.op("vector", lambda e: e.tensor_scalar(sv(1), sv(1), EPS, None, ALU.add), inc=sV)
            P.wait("vector", sV)
            P.wait("scalar", sV)
            P.op("scalar", lambda e: e.activation(sv(2), sv(1), AF.Sqrt), inc=sS)
            P.wait("vector", sS)
            P.op("vector", lambda e: e.reciprocal(sv(3), sv(2)), inc=sV)
            P.wait("vector", sV)
            P.op("vector", lambda e: e.scalar_tensor_tensor(
                sv(4), sv(0), -1.0, sv(3), ALU.mult, ALU.mult), inc=sV)
            P.wait("vector", sV)
            v_xn = 0
            for ti in range(4):
                wait_free("vector", ("xnbuf", ti % 2))
                v_xn = P.op("vector", lambda e, ti=ti, buf=(4 * grp + ti) % 7: e.tensor_scalar(
                    xnbuf[:, ti % 2, :], xbuf[:, buf, :], stat[:, ti, 3:4], stat[:, ti, 4:5],
                    ALU.mult, ALU.add), inc=sV)
                free_set(("xbuf", (4 * grp + ti) % 7), (sV, v_xn))
                P.wait("tensor", sV, v_xn)
                for k4 in range(4):
                    tb = 4 + (k4 % 2)
                    wait_free("tensor", ("bank", tb))
                    for kk in range(4):
                        k = k4 * 4 + kk
                        P.op("tensor", lambda e, k=k, kk=kk, tb=tb, ti=ti: e.transpose(
                            bank(tb)[:, kk * 128:(kk + 1) * 128],
                            xnbuf[:, ti % 2, k * 128:(k + 1) * 128], eyef[:]),
                            inc=(sT if kk == 3 else None))
                    P.wait("scalar", sT)
                    do = dst_off + ti * 128
                    v_e = P.op("scalar", lambda e, k4=k4, tb=tb, dst=dst, do=do: e.activation(
                        dst[:, k4 * 4:k4 * 4 + 4, do:do + 128],
                        bank(tb)[:].rearrange("p (a b) -> p a b", a=4), AF.Identity), inc=sV2)
                    free_set(("bank", tb), (sV2, v_e))
                free_set(("xnbuf", ti % 2), (sV2, sV2.v))
            free_set(("stat",), (sV, v_xn))

        def kv_group(src, src_off, tok0):
            for which in ("k", "v"):
                wsrc = WK if which == "k" else WV
                bsrc = bk_sb if which == "k" else bv_sb
                for m in range(4):
                    psm = bank(m)
                    wait_free("gpsimd", ("wbuf", m % 2))
                    P.op("gpsimd", lambda e, m=m, wsrc=wsrc: e.dma_start(
                        out=wbuf[:, m % 2, :, :], in_=wsrc[m]), inc=sWB[m % 2], n=16)
                    P.wait("tensor", sWB[m % 2])
                    wait_free("tensor", ("bank", m))
                    for k in range(16):
                        P.op("tensor", lambda e, m=m, k=k, psm=psm, src=src, src_off=src_off: e.matmul(
                            psm, wbuf[:, m % 2, k, :], src[:, k, src_off:src_off + 512],
                            start=(k == 0), stop=(k == 15)), inc=(sT if k == 15 else None))
                    v_mm = sT.v
                    free_set(("wbuf", m % 2), (sT, v_mm))
                    P.wait("scalar", sT, v_mm)
                    if which == "k":
                        v_ev = P.op("scalar", lambda e, m=m, psm=psm, tok0=tok0, bsrc=bsrc: e.activation(
                            kT[:, m, tok0:tok0 + 512], psm, AF.Identity, bias=bsrc[:, m:m + 1]), inc=sS)
                        free_set(("bank", m), (sS, v_ev))
                    else:
                        wait_free("scalar", ("vtmp", m % 2))
                        v_ev = P.op("scalar", lambda e, m=m, psm=psm, bsrc=bsrc: e.activation(
                            vtmp[:, m % 2, :], psm, AF.Identity, bias=bsrc[:, m:m + 1]), inc=sS)
                        free_set(("bank", m), (sS, v_ev))
                        P.wait("tensor", sS, v_ev)
                        bb = 4 + (m % 2)
                        wait_free("tensor", ("bank", bb))
                        for tb in range(4):
                            P.op("tensor", lambda e, m=m, tb=tb, bb=bb: e.transpose(
                                bank(bb)[:, tb * 128:(tb + 1) * 128],
                                vtmp[:, m % 2, tb * 128:(tb + 1) * 128], eyef[:]),
                                inc=(sT if tb == 3 else None))
                        P.wait("scalar", sT)
                        kb0 = tok0 // 128
                        v_e = P.op("scalar", lambda e, kb0=kb0, m=m, bb=bb: e.activation(
                            vaug[:, kb0:kb0 + 4, 2 * m:2 * m + 2, 0:64],
                            bank(bb)[:].rearrange("p (a c b) -> p a c b", a=4, c=2), AF.Identity), inc=sV2)
                        free_set(("bank", bb), (sV2, v_e))
                        free_set(("vtmp", m % 2), (sV2, sV2.v))

        def q_proj(qcol, ready_cnt=None, ready_v=None):
            P.wait("vector", sV, bq8_ready)
            red = []
            for t in range(16):
                wait_free("gpsimd", ("wbuf", t % 2))
                P.op("gpsimd", lambda e, t=t: e.dma_start(out=wbuf[:, t % 2, :, :], in_=WQ[t]), inc=sWB[t % 2], n=16)
                P.wait("tensor", sWB[t % 2])
                if ready_cnt is not None:
                    P.wait("tensor", ready_cnt, ready_v)
                bslot = t % 2
                pq = bank(4 + bslot)
                wait_free("tensor", ("bank", 4 + bslot))
                wait_free("tensor", ("pv", bslot))
                for k in range(16):
                    P.op("tensor", lambda e, t=t, k=k, pq=pq, qcol=qcol: e.matmul(
                        pq, wbuf[:, t % 2, k, :], xnT_h[:, k, qcol:qcol + 512],
                        start=(k == 0), stop=(k == 15)), inc=(sT if k == 15 else None))
                v_mm = sT.v
                free_set(("wbuf", t % 2), (sT, v_mm))
                P.wait("scalar", sT, v_mm)
                v_ev = P.op("scalar", lambda e, t=t, pq=pq: e.activation(
                    qT[:, t, :], pq, AF.Identity,
                    bias=bq8[:, t:t + 1], scale=0.125), inc=sS)
                free_set(("bank", 4 + bslot), (sS, v_ev))
                free_set(("pv", bslot), (sS, v_ev))
                P.wait("scalar", sS, v_ev)
                sl = t % 2
                wait_free("scalar", ("sqb", sl))
                v_sq = P.op("scalar", lambda e, t=t, sl=sl: e.activation(
                    sqb[:, sl, 0:512], qT[:, t, :], AF.Square), inc=sS)
                P.wait("tensor", sS, v_sq)
                for hf in range(2):
                    j = 2 * t + hf
                    nb = 7 if j % 2 == 0 else 6
                    wait_free("tensor", ("bank", nb))
                    if nb == 6:
                        wait_free("tensor", ("bc",))
                    v_nm = P.op("tensor", lambda e, hf=hf, sl=sl, nb=nb: e.matmul(
                        bank(nb)[0:1, :], ones_b16[64 * hf:64 * hf + 64, 0:1],
                        sqb[64 * hf:64 * hf + 64, sl, 0:512],
                        start=True, stop=True), inc=sT)
                    P.wait("vector", sT, v_nm)
                    slot = 32 + (j % 32)
                    v_r = P.op("vector", lambda e, slot=slot, nb=nb: e.tensor_reduce(
                        tiny[0:1, slot:slot + 1], bank(nb)[0:1, :],
                        axis=AX, op=ALU.max), inc=sV)
                    free_set(("bank", nb), (sV, v_r))
                    if nb == 6:
                        free_set(("bc",), (sV, v_r))
                    red.append((j, slot))
                free_set(("sqb", sl), (sT, sT.v))
            P.wait("vector", sV)
            for col, slot in red:
                P.op("vector", lambda e, col=col, s0=slot: e.tensor_copy(
                    qmax2[0:1, col:col + 1], tiny[0:1, s0:s0 + 1]), inc=sV)
            P.wait("vector", sV)
            return sS.v

        def norms(src, ntiles, ncols, dst):
            nchunks = (ncols + 511) // 512
            red = []
            for t in range(ntiles):
                for ci in range(nchunks):
                    c0 = ci * 512
                    cw = min(512, ncols - c0)
                    sl = (t * nchunks + ci) % 2
                    wait_free("scalar", ("sqb", sl))
                    v_sq = P.op("scalar", lambda e, t=t, c0=c0, cw=cw, sl=sl, src=src: e.activation(
                        sqb[:, sl, 0:cw], src[:, t, c0:c0 + cw], AF.Square), inc=sS)
                    P.wait("tensor", sS, v_sq)
                    for hf in range(2):
                        for nn in range(0, cw, 512):
                            col = 2 * t + hf
                            j = len(red) if nchunks == 1 else (t * 2 + hf + nn)
                            nb = 7 if j % 2 == 0 else 6
                            wait_free("tensor", ("bank", nb))
                            if nb == 6:
                                wait_free("tensor", ("bc",))
                            v_mm = P.op("tensor", lambda e, hf=hf, nn=nn, sl=sl, nb=nb: e.matmul(
                                bank(nb)[0:1, :], ones_b16[64 * hf:64 * hf + 64, 0:1],
                                sqb[64 * hf:64 * hf + 64, sl, nn:nn + 512],
                                start=True, stop=True), inc=sT)
                            P.wait("vector", sT, v_mm)
                            slot = 32 + (j % 32)
                            if nchunks > 1:
                                v_r = P.op("vector", lambda e, nb=nb: e.tensor_reduce(
                                    tiny[0:1, 0:1], bank(nb)[0:1, :], axis=AX, op=ALU.max), inc=sV)
                                free_set(("bank", nb), (sV, v_r))
                                if nb == 6:
                                    free_set(("bc",), (sV, v_r))
                                P.wait("vector", sV)
                                P.op("vector", lambda e, col=col, dst=dst: e.tensor_tensor(
                                    dst[0:1, col:col + 1], dst[0:1, col:col + 1], tiny[0:1, 0:1], ALU.max), inc=sV)
                                P.wait("vector", sV)
                            else:
                                v_r = P.op("vector", lambda e, slot=slot, nb=nb: e.tensor_reduce(
                                    tiny[0:1, slot:slot + 1], bank(nb)[0:1, :],
                                    axis=AX, op=ALU.max), inc=sV)
                                free_set(("bank", nb), (sV, v_r))
                                if nb == 6:
                                    free_set(("bc",), (sV, v_r))
                                red.append((col, slot))
                    free_set(("sqb", sl), (sT, sT.v))
            P.wait("vector", sV)
            by_col = {}
            for col, slot in red:
                by_col.setdefault(col, []).append(slot)
            for col, slots in by_col.items():
                if len(slots) == 1:
                    P.op("vector", lambda e, col=col, s0=slots[0], dst=dst: e.tensor_copy(
                        dst[0:1, col:col + 1], tiny[0:1, s0:s0 + 1]), inc=sV)
                else:
                    P.op("vector", lambda e, col=col, s0=slots[0], s1=slots[1], dst=dst: e.tensor_tensor(
                        dst[0:1, col:col + 1], tiny[0:1, s0:s0 + 1], tiny[0:1, s1:s1 + 1], ALU.max), inc=sV)
            P.wait("vector", sV)

        def compute_negc(qc_idx):
            P.wait("vector", sV)
            for h in range(NQ):
                t, base = q_tile_of_head(h)
                qcol = 2 * t + (1 if base else 0)
                kv = h // NG
                P.op("vector", lambda e, h=h, qcol=qcol, kv=kv: e.tensor_tensor(
                    tiny[0:1, h:h + 1], qmax2[0:1, qcol:qcol + 1], kmax2[0:1, kv:kv + 1], ALU.mult),
                    inc=(sV if h == NQ - 1 else None))
            P.wait("scalar", sV)
            P.op("scalar", lambda e: e.activation(tiny[0:1, 0:32], tiny[0:1, 0:32], AF.Sqrt), inc=sS)
            P.wait("vector", sS)
            P.op("vector", lambda e: e.tensor_scalar(tiny[0:1, 0:32], tiny[0:1, 0:32], -1.0, None, ALU.mult), inc=sV)
            P.wait("tensor", sV)
            wait_free("tensor", ("bank", 6))
            wait_free("tensor", ("bc",))
            for h in range(NQ):
                P.op("tensor", lambda e, h=h: e.matmul(
                    bank(6)[:, h:h + 1], onesf[0:1, 0:128], tiny[0:1, h:h + 1],
                    start=(h == 0), stop=(h == NQ - 1)), inc=(sT if h == NQ - 1 else None))
            P.wait("vector", sT)
            P.wait("vector", sS2)
            v_nc = P.op("vector", lambda e: e.tensor_copy(negc[:, 0:32], bank(6)[:, 0:32]), inc=sV)
            free_set(("bank", 6), (sV, v_nc))
            free_set(("bc",), (sV, v_nc))
            negc_ev[0] = v_nc
            if dbg:
                P.wait("sync", sV, v_nc)
                P.op("sync", lambda e, hi=qc_idx: e.dma_start(out=DNC[hi], in_=negc[:]), inc=sOut, n=16)

        negc_ev = [0]
        P.wait("tensor", sCstG)
        P.wait("tensor", sCst)
        tile_i = [0]
        xnT_ready = {}

        def knorm_group(grp):
            for t in range(4):
                sl = (grp * 4 + t) % 2
                wait_free("scalar", ("sqb", sl))
                v_sq = P.op("scalar", lambda e, t=t, grp=grp, sl=sl: e.activation(
                    sqb[:, sl, 0:512], kT[:, t, grp * 512:grp * 512 + 512], AF.Square), inc=sS)
                P.wait("tensor", sS, v_sq)
                for hf in range(2):
                    j = grp * 8 + t * 2 + hf
                    nb = 7 if j % 2 == 0 else 6
                    wait_free("tensor", ("bank", nb))
                    if nb == 6:
                        wait_free("tensor", ("bc",))
                    v_mm = P.op("tensor", lambda e, hf=hf, sl=sl, nb=nb: e.matmul(
                        bank(nb)[0:1, :], ones_b16[64 * hf:64 * hf + 64, 0:1],
                        sqb[64 * hf:64 * hf + 64, sl, 0:512],
                        start=True, stop=True), inc=sT)
                    P.wait("vector", sT, v_mm)
                    v_r = P.op("vector", lambda e, nb=nb: e.tensor_reduce(
                        tiny[0:1, 0:1], bank(nb)[0:1, :], axis=AX, op=ALU.max), inc=sV)
                    free_set(("bank", nb), (sV, v_r))
                    if nb == 6:
                        free_set(("bc",), (sV, v_r))
                    P.wait("vector", sV)
                    col = 2 * t + hf
                    P.op("vector", lambda e, col=col: e.tensor_tensor(
                        kmax2[0:1, col:col + 1], kmax2[0:1, col:col + 1], tiny[0:1, 0:1], ALU.max), inc=sV)
                    P.wait("vector", sV)
                free_set(("sqb", sl), (sT, sT.v))

        def do_group(grp):
            tok0 = grp * 512
            if grp < 2:
                dst, off = xnT_h, grp * 512
            else:
                dst, off = xnT_t, 0
                wait_free("vector", ("xnT_t",))
            if grp == 0:
                quad_dma(0)
            ln_quad(grp, dst, off)
            ready = sV2.v
            P.wait("tensor", sV2, ready)
            kv_group(dst, off, tok0)
            if grp < 7:
                quad_dma(grp + 1)
            ev = [(sT, sT.v)]
            if grp < 4:
                hh, c0 = grp // 2, (grp % 2) * 512
                P.wait("sync", sV2, ready)
                v_sp = P.op("sync", lambda e, hh=hh, c0=c0, dst=dst, off=off: e.dma_start(
                    out=SPXNT[hh, :, :, c0:c0 + 512], in_=dst[:, :, off:off + 512]), inc=sSp, n=16)
                ev.append((sSp, v_sp))
            if grp >= 2:
                free_set(("xnT_t",), *ev)
            xnT_ready[grp] = ready
            if grp >= 1:
                knorm_group(grp - 1)

        for grp in range(8):
            do_group(grp)
        knorm_group(7)

        if dbg:
            P.wait("sync", sS)
            P.wait("sync", sV2)
            P.op("sync", lambda e: e.dma_start(out=DKT, in_=kT[:]), inc=sOut, n=16)
            P.op("sync", lambda e: e.dma_start(out=DVA, in_=vaug[:]), inc=sOut, n=16)

        barrier()
        es1.close()

        es3 = contextlib.ExitStack()
        sbuf3 = lambda name, shape, dt: es3.enter_context(nc.sbuf_tensor(name, shape, dt))
        qT = sbuf3("qT", [128, 16, 512], BF16)
        ebl = sbuf3("ebl", [128, 2, NEBL, 512], BF16)
        etmp = sbuf3("etmp", [128, 4, 1024], BF16)
        ebig = sbuf3("ebig", [128, 3, 8, 512], BF16)
        rz = sbuf3("rz", [1, 2, 512], F32)
        rzbf = sbuf3("rzbf", [1, 2, 512], BF16)
        rzbs = sbuf3("rzbs", [64, 2, 512], BF16)
        aoT = sbuf3("aoT", [128, 16, 512], BF16)

        pvp = [pp[2][0:65, i, :] for i in range(2)]
        pvz = [pp[2][64:65, i, :] for i in range(2)]
        pvo = [pp[2][0:64, i, :] for i in range(2)]
        bcp = pp3[0:64, :]

        def pair_entries(qcbase):
            ents = sorted(kb_entries(qcbase), key=lambda x: x[1])
            out, i = [], 0
            while i < len(ents):
                if i + 1 < len(ents) and ents[i + 1][1] == ents[i][1] + 1:
                    out.append((ents[i], ents[i + 1]))
                    i += 2
                else:
                    out.append((ents[i], None))
                    i += 1
            return out

        attn_hi = [0]

        def attn_all():
            items = [(qc, (qc % 2) * 512, g) for qc in range(4) for g in range(NG)]
            ld = {}

            def load_item(j):
                slot = j % 2
                g = items[j][2]
                wait_free("gpsimd", ("ebl", slot))
                P.op("gpsimd", lambda e, g=g, slot=slot: e.dma_start(
                    out=ebl[:, slot, :, :], in_=EBL[g]), inc=sEB[slot], n=16)
                ld[j] = (sEB[slot], sEB[slot].v)

            load_item(0)
            P.wait("vector", sSp)
            for j, (qi, qcb, g) in enumerate(items):
                if j + 1 < len(items):
                    load_item(j + 1)
                if g == 0:
                    if dbg:
                        P.wait("scalar", sOut)
                    if qi == 2:
                        P.wait("sync", sT)
                        P.wait("sync", sSp)
                        P.op("sync", lambda e: e.dma_start(out=xnT_h[:], in_=SPXNT[1]), inc=sXH, n=16)
                    if qi >= 2:
                        q_proj((qi % 2) * 512, sXH, sXH.v)
                    else:
                        q_proj((qi % 2) * 512)
                    if dbg:
                        P.wait("sync", sS)
                        P.op("sync", lambda e, qi=qi: e.dma_start(out=DQT[qi], in_=qT[:]), inc=sOut, n=16)
                    compute_negc(qi)
                    wait_free("vector", ("aoT",))
                slot_e = j % 2
                prs = pair_entries(qi * 512)
                npr = len(prs)
                P.wait("tensor", ld[j][0], ld[j][1])
                P.wait("gpsimd", ld[j][0], ld[j][1])
                P.wait("vector", ld[j][0], ld[j][1])

                pair_ctr = [0]
                scinfo = {}

                def scpart(kv):
                    h = kv * NG + g
                    tq, base = q_tile_of_head(h)
                    ktile, kbase = kv // 2, 64 * (kv % 2)
                    info = []
                    for pi, ((kb_a, tix_a), eb) in enumerate(prs):
                        ss = pair_ctr[0] % 2
                        es4 = pair_ctr[0] % 4
                        pair_ctr[0] += 1
                        width = 1024 if eb else 512
                        wait_free("tensor", ("scp", ss))
                        if eb:
                            P.op("tensor", lambda e, kb=kb_a, ss=ss, ktile=ktile, kbase=kbase,
                                 tq=tq, base=base: e.matmul(
                                pp[ss][:, 0, :], kT[kbase:kbase + 64, ktile, kb * 128:(kb + 1) * 128],
                                qT[base:base + 64, tq, :], start=True, stop=True))
                            v_sc = P.op("tensor", lambda e, kb=eb[0], ss=ss, ktile=ktile, kbase=kbase,
                                        tq=tq, base=base: e.matmul(
                                pp[ss][:, 1, :], kT[kbase:kbase + 64, ktile, kb * 128:(kb + 1) * 128],
                                qT[base:base + 64, tq, :], start=True, stop=True), inc=sT)
                        else:
                            v_sc = P.op("tensor", lambda e, kb=kb_a, ss=ss, ktile=ktile, kbase=kbase,
                                        tq=tq, base=base: e.matmul(
                                pp[ss][:, 0, :], kT[kbase:kbase + 64, ktile, kb * 128:(kb + 1) * 128],
                                qT[base:base + 64, tq, :], start=True, stop=True), inc=sT)
                        P.wait("scalar", sT, v_sc)
                        P.wait("scalar", sV, negc_ev[0])
                        wait_free("scalar", ("etmp", es4))
                        v_ex = P.op("scalar", lambda e, es4=es4, ss=ss, w=width, h=h: e.activation(
                            etmp[:, es4, 0:w],
                            pp[ss][:].rearrange("p a b -> p (a b)")[:, 0:w],
                            AF.Exp, bias=negc[:, h:h + 1]), inc=sS2)
                        free_set(("scp", ss), (sS2, v_ex))
                        info.append((pi, es4, width, tix_a, eb, v_ex))
                    scinfo[kv] = info

                def multpart(kv):
                    eslot = kv % 3
                    mu_ev = []
                    for (pi, es4, width, tix_a, eb, v_ex) in scinfo.pop(kv):
                        meng, mcnt = ("gpsimd", sG) if (pi % 2 == 0) else ("vector", sV2)
                        P.wait(meng, sS2, v_ex)
                        wait_free(meng, ("ebig", eslot))
                        v_mu = P.op(meng, lambda e, es4=es4, w=width, tix=tix_a, slot_e=slot_e,
                                    eslot=eslot, pi=pi, nt=(2 if eb else 1): e.tensor_tensor(
                            ebig[:, eslot, 2 * pi:2 * pi + nt, :].rearrange("p a b -> p (a b)"),
                            etmp[:, es4, 0:w],
                            ebl[:, slot_e, tix:tix + nt, :].rearrange("p a b -> p (a b)"),
                            ALU.mult), inc=mcnt)
                        free_set(("etmp", es4), (mcnt, v_mu))
                        mu_ev.append((mcnt, v_mu))
                    return mu_ev

                def pvpart(kv, mu_ev):
                    h = kv * NG + g
                    eslot = kv % 3
                    pvs = attn_hi[0] % 2
                    for (cnt, v) in mu_ev:
                        P.wait("tensor", cnt, v)
                    wait_free("tensor", ("pv", pvs))
                    wait_free("tensor", ("bank", 4 + pvs))
                    kbs = []
                    for (kb_a, tix_a), eb in prs:
                        kbs.append(kb_a)
                        if eb:
                            kbs.append(eb[0])
                    nk = len(kbs)
                    for ki, kb in enumerate(kbs):
                        P.op("tensor", lambda e, kb=kb, kv=kv, ki=ki, eslot=eslot, pvs=pvs,
                             first=(ki == 0), last=(ki == nk - 1): e.matmul(
                            pvp[pvs], vaug[:, kb, kv, :], ebig[:, eslot, ki, :],
                            start=first, stop=last), inc=(sT2 if ki == nk - 1 else None))
                    free_set(("ebig", eslot), (sT2, sT2.v))
                    v_pvlast = sT2.v
                    P.wait("scalar", sT2, v_pvlast)
                    wait_free("scalar", ("rz", pvs))
                    P.op("scalar", lambda e, pvs=pvs: e.activation(
                        rz[0:1, pvs, :], pvz[pvs], AF.Ln), inc=sS2)
                    P.wait("scalar", sS2)
                    wait_free("scalar", ("rzbf", pvs))
                    v_rzf = P.op("scalar", lambda e, pvs=pvs: e.activation(
                        rzbf[0:1, pvs, :], rz[0:1, pvs, :], AF.Exp, scale=-1.0), inc=sS2)
                    free_set(("rz", pvs), (sS2, v_rzf))
                    pend.append((h, pvs, v_rzf))
                    attn_hi[0] += 1

                pend = []

                def flush_epi():
                    (h_p, pvs_p, v_rzf_p) = pend.pop(0)
                    P.wait("tensor", sS2, v_rzf_p)
                    wait_free("tensor", ("bc",))
                    wait_free("tensor", ("bank", 6))
                    v_bc = P.op("tensor", lambda e, pvs=pvs_p: e.matmul(
                        bcp, ones_b16[0:1, 0:64], rzbf[0:1, pvs, :], start=True, stop=True), inc=sT)
                    free_set(("rzbf", pvs_p), (sT, v_bc))
                    P.wait("vector", sT, v_bc)
                    wait_free("vector", ("rzbs", pvs_p))
                    v_cp = P.op("vector", lambda e, pvs=pvs_p: e.tensor_copy(rzbs[:, pvs, :], bcp), inc=sV)
                    free_set(("bc",), (sV, v_cp))
                    free_set(("bank", 6), (sV, v_cp))
                    P.wait("vector", sV, v_cp)
                    ta, abase = h_p // 2, 64 * (h_p % 2)
                    v_ao = P.op("vector", lambda e, ta=ta, abase=abase, pvs=pvs_p: e.tensor_tensor(
                        aoT[abase:abase + 64, ta, :], pvo[pvs], rzbs[:, pvs, :], ALU.mult), inc=sV)
                    free_set(("pv", pvs_p), (sV, v_ao))
                    free_set(("bank", 4 + pvs_p), (sV, v_ao))
                    free_set(("rzbs", pvs_p), (sV, v_ao))

                scpart(0); mu0 = multpart(0)
                scpart(1); mu1 = multpart(1)
                mu = {0: mu0, 1: mu1}
                for kv in (0, 2, 4, 6):
                    if kv + 2 < 8:
                        scpart(kv + 2)
                        mu[kv + 2] = multpart(kv + 2)
                    if kv + 3 < 8:
                        scpart(kv + 3)
                    pvpart(kv, mu.pop(kv))
                    if kv + 3 < 8:
                        mu[kv + 3] = multpart(kv + 3)
                    pvpart(kv + 1, mu.pop(kv + 1))
                    while pend:
                        flush_epi()
                while pend:
                    flush_epi()
                free_set(("ebl", slot_e), (sG, sG.v), (sV2, sV2.v))
                if g == NG - 1:
                    P.wait("sync", sV)
                    v_sp = P.op("sync", lambda e, qi=qi: e.dma_start(out=SPAOT[qi], in_=aoT[:]), inc=sSp, n=16)
                    free_set(("aoT",), (sSp, v_sp))

        attn_all()
        barrier()

        es3.close()
        es2.close()
        es4 = contextlib.ExitStack()
        sbuf4 = lambda name, shape, dt: es4.enter_context(nc.sbuf_tensor(name, shape, dt))
        woT = sbuf4("woT", [128, 16, D], BF16)
        aom = sbuf4("aom", [128, 2, 16, 512], BF16)
        sm = sbuf4("sm", [128, 2, D], BF16)

        P.wait("gpsimd", sSp)
        if dbg:
            P.wait("gpsimd", sOut)
            P.wait("sync", sOut)
        for k in range(16):
            P.op("gpsimd", lambda e, k=k: e.dma_start(out=woT[:, k, :], in_=WO[k]), inc=sCstG, n=16)
        P.wait("tensor", sCstG)
        P.wait("sync", sSp)
        P.op("sync", lambda e: e.dma_start(out=aom[:, 0, :, :], in_=SPAOT[0]), inc=sAM[0], n=16)
        for qi in range(4):
            if qi + 1 < 4:
                wait_free("sync", ("aom", (qi + 1) % 2))
                P.op("sync", lambda e, qi=qi: e.dma_start(
                    out=aom[:, (qi + 1) % 2, :, :], in_=SPAOT[qi + 1]), inc=sAM[(qi + 1) % 2], n=16)
            P.wait("tensor", sAM[qi % 2])
            for mm_ in range(4):
                m = qi * 4 + mm_
                for n in range(4):
                    pso = bank(n)
                    wait_free("tensor", ("bank", n))
                    for k in range(16):
                        P.op("tensor", lambda e, qi=qi, mm_=mm_, n=n, k=k, pso=pso: e.matmul(
                            pso, aom[:, qi % 2, k, mm_ * 128:(mm_ + 1) * 128],
                            woT[:, k, n * 512:(n + 1) * 512],
                            start=(k == 0), stop=(k == 15)), inc=(sT if k == 15 else None))
                    v_mm = sT.v
                    P.wait("scalar", sT, v_mm)
                    if n == 0:
                        wait_free("scalar", ("sm", m % 2))
                    v_ev = P.op("scalar", lambda e, m=m, n=n, pso=pso: e.activation(
                        sm[:, m % 2, n * 512:(n + 1) * 512], pso, AF.Identity), inc=sS)
                    free_set(("bank", n), (sS, v_ev))
                if mm_ == 3:
                    free_set(("aom", qi % 2), (sT, sT.v))
                P.wait("sync", sS)
                v_sp = P.op("sync", lambda e, m=m: e.dma_start(out=SPS[m], in_=sm[:, m % 2, :]), inc=sSm[m % 2], n=16)
                free_set(("sm", m % 2), (sSm[m % 2], v_sp))

        barrier()
        bank8 = bank

        def pair(i):
            return pp[i][:].rearrange("p a b -> p (a b)")

        es4.close()
        P.wait("gpsimd", sSm[0])
        P.wait("gpsimd", sSm[1])
        P.wait("sync", sSm[0])
        P.wait("sync", sSm[1])
        rtr = sbuf("rtr", [128, 16, 8], BF16)
        lgf = sbuf("lgf", [8, 1024], F32)
        logT = sbuf("logT", [128, 8, 8], F32)
        idx8 = sbuf("idx8", [128, 8], F32)
        idxB = sbuf("idxB", [128, 8], F32)
        msk = sbuf("msk", [128, 8], F32)
        m1k = sbuf("m1k", [128, 8], F32)
        m2k = sbuf("m2k", [128, 8], F32)
        tmp8 = sbuf("tmp8", [128, 8], F32)
        r6 = sbuf("r6", [128, 6], F32)
        combT = sbuf("combT", [128, 8, 8], F32)
        gupw = sbuf("gupw", [128, 2, 16, 256], BF16)
        dwnw = sbuf("dwnw", [128, 2, 11, 512], BF16)
        he = sbuf("he", [128, 11, HALF], BF16)
        silu_t = sbuf("silu_t", [128, 2, 1024], BF16)
        acc = sbuf("acc", [128, 8, D], F32)
        gbuf = sbuf("gbuf", [128, D], F32)
        sob = sbuf("sob", [128, 2, D], BF16)
        sobf = sbuf("sobf", [128, D], F32)
        outb = sbuf("outb", [128, 2, D], F32)

        P.op("gpsimd", lambda e: e.dma_start(out=rtr[:], in_=RTR), inc=sCstG, n=16)
        v_rtr = sCstG.v
        v_iota = P.op("gpsimd", lambda e: e.iota(idx8[:], pattern=[[1, 8]], base=0, channel_multiplier=0,
                                                 allow_small_or_imprecise_dtypes=True), inc=sG)
        BIGC = 65536.0
        P.wait("vector", sG, v_iota)
        P.op("vector", lambda e: e.tensor_scalar(idxB[:], idx8[:], BIGC, None, ALU.add), inc=sV)

        def vop(fn):
            P.op("vector", fn, inc=sV)
            P.wait("vector", sV)

        def router(hh):
            if dbg:
                P.wait("vector", sOut)
            P.wait("tensor", sCstG, v_rtr)
            for n in range(2):
                wait_free("tensor", ("dbank", 4))
                for k in range(16):
                    P.op("tensor", lambda e, n=n, k=k: e.matmul(
                        bank8(4)[0:8, :], rtr[:, k, :], xnT_h[:, k, n * 512:(n + 1) * 512],
                        start=(k == 0), stop=(k == 15)), inc=(sT if k == 15 else None))
                P.wait("vector", sT)
                v_ev = P.op("vector", lambda e, n=n: e.tensor_copy(
                    lgf[0:8, n * 512:(n + 1) * 512], bank8(4)[0:8, :]), inc=sV)
                free_set(("dbank", 4), (sV, v_ev))
            P.wait("tensor", sV)
            for tb in range(8):
                wait_free("tensor", ("dbank", 5))
                v_t = P.op("tensor", lambda e, tb=tb: e.transpose(
                    bank8(5)[0:128, 0:8], lgf[0:8, tb * 128:(tb + 1) * 128], eyef[0:8, 0:8]), inc=sT)
                P.wait("vector", sT, v_t)
                v_e = P.op("vector", lambda e, tb=tb: e.tensor_copy(logT[:, tb, :], bank8(5)[0:128, 0:8]), inc=sV)
                free_set(("dbank", 5), (sV, v_e))
            for tb in range(8):
                L = logT[:, tb, :]
                vop(lambda e, L=L: e.tensor_reduce(r6[:, 0:1], L, axis=AX, op=ALU.max))
                vop(lambda e, L=L: e.tensor_scalar(msk[:], L, r6[:, 0:1], None, ALU.is_equal))
                vop(lambda e: e.scalar_tensor_tensor(tmp8[:], msk[:], -BIGC, idxB[:], ALU.mult, ALU.add))
                vop(lambda e: e.tensor_reduce(r6[:, 1:2], tmp8[:], axis=AX, op=ALU.min))
                vop(lambda e: e.tensor_scalar(m1k[:], idx8[:], r6[:, 1:2], None, ALU.is_equal))
                vop(lambda e, L=L: e.scalar_tensor_tensor(tmp8[:], m1k[:], -BIGC, L, ALU.mult, ALU.add))
                vop(lambda e: e.tensor_reduce(r6[:, 2:3], tmp8[:], axis=AX, op=ALU.max))
                vop(lambda e: e.tensor_scalar(msk[:], tmp8[:], r6[:, 2:3], None, ALU.is_equal))
                vop(lambda e: e.scalar_tensor_tensor(tmp8[:], msk[:], -BIGC, idxB[:], ALU.mult, ALU.add))
                vop(lambda e: e.tensor_reduce(r6[:, 3:4], tmp8[:], axis=AX, op=ALU.min))
                vop(lambda e: e.tensor_scalar(m2k[:], idx8[:], r6[:, 3:4], None, ALU.is_equal))
                vop(lambda e: e.tensor_tensor(r6[:, 4:5], r6[:, 0:1], r6[:, 2:3], ALU.subtract))
                P.wait("scalar", sV)
                P.op("scalar", lambda e: e.activation(r6[:, 4:5], r6[:, 4:5], AF.Sigmoid), inc=sS)
                P.wait("vector", sS)
                vop(lambda e: e.tensor_scalar(r6[:, 5:6], r6[:, 4:5], -1.0, 1.0, ALU.mult, ALU.add))
                vop(lambda e: e.tensor_scalar(m1k[:], m1k[:], r6[:, 4:5], None, ALU.mult))
                vop(lambda e: e.tensor_scalar(m2k[:], m2k[:], r6[:, 5:6], None, ALU.mult))
                vop(lambda e, tb=tb: e.tensor_tensor(combT[:, tb, :], m1k[:], m2k[:], ALU.add))
            if dbg:
                P.wait("sync", sV)
                P.op("sync", lambda e, hh=hh: e.dma_start(out=DCB[hh], in_=combT[:]), inc=sOut, n=16)

        def moe_experts(hh, interleave=None):
            for e_ in range(NE):
                for hg in range(11):
                    if interleave:
                        interleave.pop(0)()
                    sl = hg % 2
                    wait_free("gpsimd", ("gupw", sl))
                    P.op("gpsimd", lambda e, e_=e_, hg=hg, sl=sl: e.dma_start(
                        out=gupw[:, sl, :, :], in_=GUP[e_, hg]), inc=sGW[sl], n=16)
                    P.wait("tensor", sGW[sl])
                    gb = 0 if hg % 2 == 0 else 1
                    ub0, ub1 = (4, 5) if hg % 2 == 0 else (6, 7)
                    for db in (2 * gb, 2 * gb + 1, ub0, ub1):
                        wait_free("tensor", ("dbank", db))
                    for k in range(16):
                        P.op("tensor", lambda e, sl=sl, k=k, gb=gb: e.matmul(
                            pp[gb][:, 0, :], gupw[:, sl, k, 0:128], xnT_h[:, k, 0:512],
                            start=(k == 0), stop=(k == 15)))
                        P.op("tensor", lambda e, sl=sl, k=k, gb=gb: e.matmul(
                            pp[gb][:, 1, :], gupw[:, sl, k, 0:128], xnT_h[:, k, 512:1024],
                            start=(k == 0), stop=(k == 15)))
                        P.op("tensor", lambda e, sl=sl, k=k, ub0=ub0: e.matmul(
                            bank8(ub0), gupw[:, sl, k, 128:256], xnT_h[:, k, 0:512],
                            start=(k == 0), stop=(k == 15)))
                        P.op("tensor", lambda e, sl=sl, k=k, ub1=ub1: e.matmul(
                            bank8(ub1), gupw[:, sl, k, 128:256], xnT_h[:, k, 512:1024],
                            start=(k == 0), stop=(k == 15)), inc=(sT if k == 15 else None))
                    v_mm = sT.v
                    free_set(("gupw", sl), (sT, v_mm))
                    P.wait("scalar", sT, v_mm)
                    wait_free("scalar", ("silu", sl))
                    v_si = P.op("scalar", lambda e, sl=sl, gb=gb: e.activation(
                        silu_t[:, sl, :], pair(gb), AF.Silu), inc=sS)
                    free_set(("dbank", 2 * gb), (sS, v_si))
                    free_set(("dbank", 2 * gb + 1), (sS, v_si))
                    P.wait("vector", sS, v_si)
                    wait_free("vector", ("he", hg))
                    if hg % 2 == 0:
                        v_he = P.op("vector", lambda e, sl=sl, hg=hg: e.tensor_tensor(
                            he[:, hg, :], silu_t[:, sl, :], pair(2), ALU.mult), inc=sV2)
                        free_set(("dbank", 4), (sV2, v_he))
                        free_set(("dbank", 5), (sV2, v_he))
                    else:
                        P.op("vector", lambda e, sl=sl, hg=hg: e.tensor_tensor(
                            he[:, hg, 0:512], silu_t[:, sl, 0:512], bank8(6), ALU.mult), inc=sV2)
                        v_he = P.op("vector", lambda e, sl=sl, hg=hg: e.tensor_tensor(
                            he[:, hg, 512:1024], silu_t[:, sl, 512:1024], bank8(7), ALU.mult), inc=sV2)
                        free_set(("dbank", 6), (sV2, v_he))
                        free_set(("dbank", 7), (sV2, v_he))
                    free_set(("silu", sl), (sV2, v_he))
                v_he_all = sV2.v
                if e_ == NE - 1:
                    free_set(("xnT_h",), (sT, sT.v))
                P.wait("tensor", sV2, v_he_all)
                for n in range(4):
                    sl = n % 2
                    wait_free("gpsimd", ("dwnw", sl))
                    P.op("gpsimd", lambda e, e_=e_, n=n, sl=sl: e.dma_start(
                        out=dwnw[:, sl, :, :], in_=DWN[e_, n]), inc=sDW[sl], n=16)
                    P.wait("tensor", sDW[sl])
                    for mh in range(2):
                        ms = range(4 * mh, 4 * mh + 4)
                        for m in ms:
                            wait_free("tensor", ("dbank", m))
                        for k in range(11):
                            for m in ms:
                                P.op("tensor", lambda e, sl=sl, k=k, m=m: e.matmul(
                                    bank8(m), he[:, k, m * 128:(m + 1) * 128], dwnw[:, sl, k, :],
                                    start=(k == 0), stop=(k == 10)),
                                    inc=(sT if (k == 10 and m == 4 * mh + 3) else None))
                        v_mm = sT.v
                        if mh == 1:
                            free_set(("dwnw", sl), (sT, v_mm))
                        P.wait("vector", sT, v_mm)
                        for m in ms:
                            cap = combT[:, m, e_:e_ + 1]
                            asl = acc[:, m, n * 512:(n + 1) * 512]
                            if e_ == 0:
                                wait_free("vector", ("accm", m))
                                v_up = P.op("vector", lambda e, m=m, cap=cap, asl=asl: e.tensor_scalar(
                                    asl, bank8(m), cap, None, ALU.mult), inc=sV)
                            else:
                                v_up = P.op("vector", lambda e, m=m, cap=cap, asl=asl: e.scalar_tensor_tensor(
                                    asl, bank8(m), cap, asl, ALU.mult, ALU.add), inc=sV)
                            free_set(("dbank", m), (sV, v_up))
                for hg in range(11):
                    free_set(("he", hg), (sT, sT.v))
        def moe_combine_units(hh):
            units = []
            v_acc_done = sV.v
            for m in range(8):
                units.append(lambda m=m, hh=hh, v_acc_done=v_acc_done: _combine_m(hh, m, v_acc_done))
            return units

        def _combine_m(hh, m, v_acc_done):
            if True:
                P.wait("scalar", sV, v_acc_done)
                wait_free("scalar", ("gbuf",))
                v_g = P.op("scalar", lambda e, m=m: e.activation(gbuf[:], acc[:, m, :], AF.Silu), inc=sS)
                free_set(("accm", m), (sS, v_g))
                rows = hh * HALF + m * 128
                wait_free("sync", ("outb", m % 2))
                P.op("sync", lambda e, rows=rows, m=m: e.dma_start(
                    out=outb[:, m % 2, :], in_=X[rows:rows + 128, :]), inc=sXL[m % 2], n=16)
                wait_free("sync", ("sob", m % 2))
                P.wait("sync", sSm[0])
                P.wait("sync", sSm[1])
                P.op("sync", lambda e, hh=hh, m=m: e.dma_start(
                    out=sob[:, m % 2, :], in_=SPS[hh * 8 + m]), inc=sSB[m % 2], n=16)
                P.wait("vector", sXL[m % 2])
                P.wait("vector", sSB[m % 2])
                P.wait("vector", sS, v_g)
                wait_free("vector", ("sobf",))
                v_sf = P.op("vector", lambda e, m=m: e.tensor_copy(sobf[:], sob[:, m % 2, :]), inc=sV)
                free_set(("sob", m % 2), (sV, v_sf))
                P.wait("vector", sV, v_sf)
                v_o1 = P.op("vector", lambda e, m=m: e.tensor_tensor(
                    gbuf[:], sobf[:], gbuf[:], ALU.mult), inc=sV)
                free_set(("sobf",), (sV, v_o1))
                P.wait("vector", sV, v_o1)
                v_o2 = P.op("vector", lambda e, m=m: e.tensor_tensor(
                    outb[:, m % 2, :], outb[:, m % 2, :], gbuf[:], ALU.add), inc=sV)
                free_set(("gbuf",), (sV, v_o2))
                P.wait("sync", sV, v_o2)
                v_st = P.op("sync", lambda e, rows=rows, m=m: e.dma_start(
                    out=OUT[rows:rows + 128, :], in_=outb[:, m % 2, :]), inc=sO[m % 2], n=16)
                free_set(("outb", m % 2), (sO[m % 2], v_st))

        wait_free("sync", ("xnT_h",))
        P.wait("sync", sSp)
        P.op("sync", lambda e: e.dma_start(out=xnT_h[:], in_=SPXNT[0]), inc=sXH, n=16)
        P.wait("tensor", sXH)
        router(0)
        moe_experts(0)
        wait_free("sync", ("xnT_h",))
        P.op("sync", lambda e: e.dma_start(out=xnT_h[:], in_=SPXNT[1]), inc=sXH, n=16)
        P.wait("tensor", sXH)
        P.wait("vector", sV)
        router(1)
        moe_experts(1, interleave=moe_combine_units(0))
        for u in moe_combine_units(1):
            u()

        P.wait("sync", sO[0])
        P.wait("sync", sO[1])
        P.wait("sync", sSp)
        P.wait("sync", sSm[0])
        P.wait("sync", sSm[1])
        if dbg:
            P.wait("sync", sOut)

        block = es.enter_context(nc.Block())
        for eng_name in Plan.ENGS:
            fns = P.streams[eng_name]
            getattr(block, eng_name)(lambda e, fns=fns: [fn(e) for fn in fns])

    return nc



_CACHE = {}


def _packs(inp):
    import ml_dtypes
    f32 = np.float32
    Wq = np.asarray(inp["L0_attn_Wq"], f32)
    Wk = np.asarray(inp["L0_attn_Wk"], f32)
    Wv = np.asarray(inp["L0_attn_Wv"], f32)
    Wo = np.asarray(inp["L0_attn_Wo"], f32)
    bq = np.asarray(inp["L0_attn_bq"], f32)
    bk = np.asarray(inp["L0_attn_bk"], f32)
    bv = np.asarray(inp["L0_attn_bv"], f32)
    gup = np.asarray(inp["L0_ffn_gate_up_proj"], f32)
    dwn = np.asarray(inp["L0_ffn_down_proj"], f32)
    rtr = np.asarray(inp["L0_ffn_router_w"], f32)

    WQp = np.empty((16, 128, 16, 128), f32)
    BQp = np.empty((16, 128), f32)
    WqT = Wq.reshape(2048, 16, 128).transpose(1, 2, 0)
    for t in range(16):
        A = 8 * (t // 4) + (t % 4)
        feats = np.concatenate([A * 64 + np.arange(64), (A + 4) * 64 + np.arange(64)])
        WQp[t] = WqT[:, :, feats].transpose(1, 0, 2)
        BQp[t] = bq[feats]
    WKp = np.ascontiguousarray(Wk.reshape(4, 128, 16, 128).transpose(0, 3, 2, 1))
    WVp = np.ascontiguousarray(Wv.reshape(4, 128, 16, 128).transpose(0, 3, 2, 1))
    BKp = np.ascontiguousarray(bk.reshape(4, 128))
    BVp = np.ascontiguousarray(bv.reshape(4, 128))
    WOp = np.ascontiguousarray(Wo.T.reshape(16, 128, 2048))

    gate = gup[:, :NH, :].reshape(NE, 11, 128, 16, 128)
    up = gup[:, NH:, :].reshape(NE, 11, 128, 16, 128)
    GUPp = np.ascontiguousarray(
        np.concatenate([gate.transpose(0, 1, 4, 3, 2), up.transpose(0, 1, 4, 3, 2)], axis=4))
    DWNp = np.ascontiguousarray(
        dwn.transpose(0, 2, 1).reshape(NE, 11, 128, 4, 512).transpose(0, 3, 2, 1, 4))
    RTRp = np.ascontiguousarray(rtr.T.reshape(16, 128, 8).transpose(1, 0, 2))
    EYEp = np.eye(128, dtype=f32)

    i_ = np.arange(512)[None, :]
    p_ = np.arange(128)[:, None]
    ebls = []
    for f in (0, 1):
        tbl = np.zeros((NG, 128, NEBL, 512), np.float32)
        for ti in range(NEBL):
            if ti < len(OWN_SIGMAS):
                sig, wrap = OWN_SIGMAS[ti], 0
            else:
                sig, wrap = OTHER_SIGMAS[ti - len(OWN_SIGMAS)], 4096 * f
            dist = np.abs(i_ - p_ + sig + wrap).astype(np.float32)
            for g in range(NG):
                tbl[g, :, ti, :] = np.exp(-SLOPES[g] * dist)
        ebls.append(tbl)

    shared = dict(WQ=WQp, WK=WKp, WV=WVp, WO=WOp, BQ=BQp, BK=BKp, BV=BVp,
                  GUP=GUPp, DWN=DWNp, RTR=RTRp, EYE=EYEp)
    return shared, ebls


def make_in_maps(inp):
    x = np.asarray(inp["L0_attn_hidden_states"], np.float32)
    shared, ebls = _packs(inp)
    in_maps = []
    for c in range(8):
        b, h = c // 2, c % 2
        Xc = x[b] if h == 0 else np.ascontiguousarray(np.roll(x[b], -OWN, axis=0))
        m = dict(shared)
        m["X"] = Xc
        m["EBL"] = ebls[h]
        in_maps.append(m)
    return in_maps


def kernel(**inp):
    if "nc" not in _CACHE:
        _CACHE["nc"] = build_program(dbg=False)
    nc = _CACHE["nc"]
    in_maps = make_in_maps(inp)
    res = run_bass_kernel_spmd(nc, in_maps, core_ids=list(range(8)))
    out = np.empty((NB, S, D), np.float32)
    for c in range(8):
        b, h = c // 2, c % 2
        out[b, h * OWN:(h + 1) * OWN] = res.results[c]["OUT"]
    return out


# version 19 (final)
# speedup vs baseline: 1.0125x; 1.0125x over previous
import contextlib
import numpy as np

import concourse.bass as bass
import concourse.mybir as mybir
from concourse.bass_utils import run_bass_kernel_spmd

F32 = mybir.dt.float32
BF16 = mybir.dt.bfloat16
AF = mybir.ActivationFunctionType
ALU = mybir.AluOpType
AX = mybir.AxisListType.X

D = 2048
S = 4096
NB = 4
OWN = 2048
HALF = 1024
NKV = 8
NQ = 32
NG = 4
NE = 8
NH = 1408
EPS = 1e-5
SLOPES = [2.0 ** (-g / NG) for g in range(NG)]

OWN_SIGMAS = list(range(-640, 257, 128))
OTHER_SIGMAS = [-640, -512, -3968, -3840]
NEBL = len(OWN_SIGMAS) + len(OTHER_SIGMAS)


def kb_entries(qcbase):
    out = []
    for kb in range(32):
        sig = qcbase - kb * 128
        if kb < 16 and sig in OWN_SIGMAS:
            out.append((kb, OWN_SIGMAS.index(sig)))
        elif kb >= 16 and sig in OTHER_SIGMAS:
            out.append((kb, len(OWN_SIGMAS) + OTHER_SIGMAS.index(sig)))
    return out


def q_tile_of_head(h):
    kv, g = h // NG, h % NG
    return (kv // 2) * NG + g, 64 * (kv % 2)


class Cnt:
    def __init__(self, es, nc, name):
        self.h = es.enter_context(nc.semaphore(name))
        self.v = 0

    def bump(self, n=1):
        self.v += n
        return self.v


class Plan:
    ENGS = ("sync", "tensor", "scalar", "vector", "gpsimd")

    def __init__(self):
        self.streams = {e: [] for e in self.ENGS}
        self.seen = {e: {} for e in self.ENGS}

    def wait(self, eng, cnt, v=None):
        val = cnt.v if v is None else v
        if val <= 0:
            return
        if self.seen[eng].get(id(cnt), 0) >= val:
            return
        self.seen[eng][id(cnt)] = val
        self.streams[eng].append(lambda e, h=cnt.h, val=val: e.wait_ge(h, val))

    def op(self, eng, fn, inc=None, n=1):
        if inc is None:
            self.streams[eng].append(fn)
            return None
        v = inc.bump(n)
        self.streams[eng].append(lambda e, fn=fn, h=inc.h, n=n: fn(e).then_inc(h, n))
        return v


def build_program(dbg=False):
    nc = bass.Bass("TRN2", target_bir_lowering=False)

    X = nc.dram_tensor("X", [S, D], F32, kind="ExternalInput").ap()
    WQ = nc.dram_tensor("WQ", [16, 128, 16, 128], F32, kind="ExternalInput").ap()
    WK = nc.dram_tensor("WK", [4, 128, 16, 128], F32, kind="ExternalInput").ap()
    WV = nc.dram_tensor("WV", [4, 128, 16, 128], F32, kind="ExternalInput").ap()
    WO = nc.dram_tensor("WO", [16, 128, D], F32, kind="ExternalInput").ap()
    BQ = nc.dram_tensor("BQ", [16, 128], F32, kind="ExternalInput").ap()
    BK = nc.dram_tensor("BK", [4, 128], F32, kind="ExternalInput").ap()
    BV = nc.dram_tensor("BV", [4, 128], F32, kind="ExternalInput").ap()
    GUP = nc.dram_tensor("GUP", [NE, 11, 128, 16, 256], F32, kind="ExternalInput").ap()
    DWN = nc.dram_tensor("DWN", [NE, 4, 128, 11, 512], F32, kind="ExternalInput").ap()
    RTR = nc.dram_tensor("RTR", [128, 16, 8], F32, kind="ExternalInput").ap()
    EYE = nc.dram_tensor("EYE", [128, 128], F32, kind="ExternalInput").ap()
    SEL = nc.dram_tensor("SEL", [8, 32], F32, kind="ExternalInput").ap()
    EBL = nc.dram_tensor("EBL", [NG, 128, NEBL, 512], F32, kind="ExternalInput").ap()

    OUT = nc.dram_tensor("OUT", [OWN, D], F32, kind="ExternalOutput").ap()

    spk = "ExternalOutput" if dbg else "Internal"
    SPXNT = nc.dram_tensor("SPXNT", [2, 128, 16, HALF], BF16, kind=spk).ap()
    SPAOT = nc.dram_tensor("SPAOT", [4, 128, 16, 512], BF16, kind=spk).ap()
    SPS = nc.dram_tensor("SPS", [16, 128, D], BF16, kind=spk).ap()
    if dbg:
        DKT = nc.dram_tensor("DKT", [128, 4, S], BF16, kind="ExternalOutput").ap()
        DVA = nc.dram_tensor("DVA", [128, 32, 8, 65], BF16, kind="ExternalOutput").ap()
        DQT = nc.dram_tensor("DQT", [4, 128, 16, 512], BF16, kind="ExternalOutput").ap()
        DNC = nc.dram_tensor("DNC", [4, 128, 32], F32, kind="ExternalOutput").ap()
        DCB = nc.dram_tensor("DCB", [2, 128, 8, 8], F32, kind="ExternalOutput").ap()

    P = Plan()
    es = contextlib.ExitStack()
    free = {}

    def free_set(key, *events):
        free[key] = list(events)

    def wait_free(eng, key):
        for (cnt, v) in free.get(key, ()):
            P.wait(eng, cnt, v)

    with es:
        sbuf = lambda name, shape, dt: es.enter_context(nc.sbuf_tensor(name, shape, dt))

        eye = sbuf("eye", [128, 128], BF16)
        eyef = sbuf("eyef", [128, 128], F32)
        onesf = sbuf("onesf", [128, 128], F32)
        ones_b16 = sbuf("ones_b16", [128, 64], BF16)
        bq8 = sbuf("bq8", [128, 16], F32)
        bk_sb = sbuf("bk_sb", [128, 4], F32)
        bv_sb = sbuf("bv_sb", [128, 4], F32)
        negc = sbuf("negc", [128, 32], F32)
        tiny = sbuf("tiny", [1, 64], F32)
        ohk = sbuf("ohk", [128, 4, 8], BF16)
        sel = sbuf("sel", [8, 32], F32)
        kmaxg = sbuf("kmaxg", [8, 9], F32)
        kqb = sbuf("kqb", [32, 1], F32)
        qn = sbuf("qn", [32, 2], F32)
        xnT_h = sbuf("xnT_h", [128, 16, HALF], BF16)
        es2 = contextlib.ExitStack()
        sbuf2 = lambda name, shape, dt: es2.enter_context(nc.sbuf_tensor(name, shape, dt))
        kT = sbuf2("kT", [128, 4, S], BF16)
        vaug = sbuf2("vaug", [128, 32, 8, 65], BF16)
        wbuf = sbuf2("wbuf", [128, 2, 16, 128], BF16)
        sqb = sbuf2("sqb", [128, 2, 512], BF16)

        sCst = Cnt(es, nc, "sCst")
        sCstG = Cnt(es, nc, "sCstG")
        sXB = [Cnt(es, nc, f"sXB{i}") for i in range(2)]
        sXB2 = [Cnt(es, nc, f"sXB2{i}") for i in range(2)]
        sXB3 = [Cnt(es, nc, f"sXB3{i}") for i in range(2)]
        sXB4 = [Cnt(es, nc, f"sXB4{i}") for i in range(2)]
        sWB = [Cnt(es, nc, f"sWB{i}") for i in range(2)]
        sEB = [Cnt(es, nc, f"sEB{i}") for i in range(2)]
        sAM = [Cnt(es, nc, f"sAM{i}") for i in range(2)]
        sGW = [Cnt(es, nc, f"sGW{i}") for i in range(2)]
        sDW = [Cnt(es, nc, f"sDW{i}") for i in range(2)]
        sSB = [Cnt(es, nc, f"sSB{i}") for i in range(2)]
        sXL = [Cnt(es, nc, f"sXL{i}") for i in range(2)]
        sO = [Cnt(es, nc, f"sO{i}") for i in range(2)]
        sXH = Cnt(es, nc, "sXH")
        sSm = [Cnt(es, nc, f"sSm{i}") for i in range(2)]
        sSp = Cnt(es, nc, "sSp")
        sT = Cnt(es, nc, "sT")
        sT2 = Cnt(es, nc, "sT2")
        sS = Cnt(es, nc, "sS")
        sS2 = Cnt(es, nc, "sS2")
        sV = Cnt(es, nc, "sV")
        sV2 = Cnt(es, nc, "sV2")
        sG = Cnt(es, nc, "sG")
        sOut = Cnt(es, nc, "sOut")
        sBar = Cnt(es, nc, "sBar")

        def barrier():
            base = sBar.v
            for eng in Plan.ENGS:
                P.op(eng, lambda e: e.nop(), inc=sBar)
            for eng in Plan.ENGS:
                P.wait(eng, sBar, base + 5)
            free.clear()

        P.op("gpsimd", lambda e: e.dma_start(out=eye[:], in_=EYE), inc=sCstG, n=16)
        P.op("sync", lambda e: e.dma_start(out=eyef[:], in_=EYE), inc=sCst, n=16)
        P.op("sync", lambda e: e.dma_start(out=sel[:], in_=SEL), inc=sCst, n=16)
        for t in range(16):
            P.op("sync", lambda e, t=t: e.dma_start(
                out=bq8[:, t:t + 1], in_=BQ[t:t + 1, :].rearrange("o p -> p o")), inc=sCst, n=16)
        for t in range(4):
            P.op("sync", lambda e, t=t: e.dma_start(
                out=bk_sb[:, t:t + 1], in_=BK[t:t + 1, :].rearrange("o p -> p o")), inc=sCst, n=16)
            P.op("sync", lambda e, t=t: e.dma_start(
                out=bv_sb[:, t:t + 1], in_=BV[t:t + 1, :].rearrange("o p -> p o")), inc=sCst, n=16)
        P.wait("vector", sCst)
        P.wait("vector", sCstG)
        P.op("vector", lambda e: e.memset(onesf[:], 1.0), inc=sV)
        P.op("vector", lambda e: e.memset(ones_b16[:], 1.0), inc=sV)
        P.op("vector", lambda e: e.memset(vaug[:, :, :, 64:65], 1.0), inc=sV)
        P.op("vector", lambda e: e.memset(ohk[:], 0.0), inc=sV)
        P.wait("vector", sV)
        for t in range(4):
            P.op("vector", lambda e, t=t: e.memset(ohk[0:64, t, 2 * t:2 * t + 1], 1.0), inc=sV)
            P.op("vector", lambda e, t=t: e.memset(ohk[64:128, t, 2 * t + 1:2 * t + 2], 1.0), inc=sV)
        P.op("vector", lambda e: e.tensor_scalar(bq8[:], bq8[:], 0.125, None, ALU.mult), inc=sV)
        bq8_ready = sV.v

        es1 = contextlib.ExitStack()
        sbuf1 = lambda name, shape, dt: es1.enter_context(nc.sbuf_tensor(name, shape, dt))
        xbuf = sbuf1("xbuf", [128, 7, D], F32)
        xnbuf = sbuf1("xnbuf", [128, 2, D], F32)
        st6 = sbuf1("st6", [128, 4, 4, 6], F32)
        stat = sbuf1("stat", [128, 4, 8], F32)
        xnT_t = sbuf1("xnT_t", [128, 16, 512], BF16)
        vtmp = sbuf1("vtmp", [128, 2, 512], F32)

        pp = [es.enter_context(nc.psum_tensor(f"pp{i}", [128, 2, 512], F32)) for i in range(3)]
        pp3 = es.enter_context(nc.psum_tensor("pp3", [128, 512], F32))
        pp4 = es.enter_context(nc.psum_tensor("pp4", [128, 512], F32))

        def bank(i):
            if i < 6:
                return pp[i // 2][:, i % 2, :]
            return pp3[:] if i == 6 else pp4[:]

        xq_v = {}
        S7 = sXB + sXB2 + sXB3 + sXB4

        def quad_dma(grp):
            rows0 = grp * 512
            for ti in range(4):
                buf = (4 * grp + ti) % 7
                q = "sync" if ti < 2 else "scalar"
                wait_free(q, ("xbuf", buf))
                v = P.op(q, lambda e, r0=rows0 + ti * 128, buf=buf: e.dma_start(
                    out=xbuf[:, buf, :], in_=X[r0:r0 + 128, :]), inc=S7[buf], n=16)
                xq_v[(grp, ti)] = (S7[buf], v)

        def ln_quad(grp, dst, dst_off):
            wait_free("vector", ("stat",))
            for ti in range(4):
                buf = (4 * grp + ti) % 7
                P.wait("vector", *xq_v.pop((grp, ti)))
                for i in range(4):
                    P.op("vector", lambda e, ti=ti, i=i, buf=buf: e.bn_stats(
                        st6[:, ti, i, :], xbuf[:, buf, i * 512:(i + 1) * 512]),
                        inc=(sV if i == 3 else None))
            P.wait("vector", sV)
            for ti in range(4):
                P.op("vector", lambda e, ti=ti: e.bn_aggr(
                    stat[:, ti, 0:2], st6[:, ti].rearrange("p a b -> p (a b)")),
                    inc=(sV if ti == 3 else None))
            P.wait("vector", sV)
            sv = lambda c: stat[:, :, c:c + 1].rearrange("p a o -> p (a o)")
            P.op("vector", lambda e: e.tensor_scalar(sv(1), sv(1), EPS, None, ALU.add), inc=sV)
            P.wait("vector", sV)
            P.wait("scalar", sV)
            P.op("scalar", lambda e: e.activation(sv(2), sv(1), AF.Sqrt), inc=sS)
            P.wait("vector", sS)
            P.op("vector", lambda e: e.reciprocal(sv(3), sv(2)), inc=sV)
            P.wait("vector", sV)
            P.op("vector", lambda e: e.scalar_tensor_tensor(
                sv(4), sv(0), -1.0, sv(3), ALU.mult, ALU.mult), inc=sV)
            P.wait("vector", sV)
            v_xn = 0
            for ti in range(4):
                wait_free("vector", ("xnbuf", ti % 2))
                v_xn = P.op("vector", lambda e, ti=ti, buf=(4 * grp + ti) % 7: e.tensor_scalar(
                    xnbuf[:, ti % 2, :], xbuf[:, buf, :], stat[:, ti, 3:4], stat[:, ti, 4:5],
                    ALU.mult, ALU.add), inc=sV)
                free_set(("xbuf", (4 * grp + ti) % 7), (sV, v_xn))
                P.wait("tensor", sV, v_xn)
                for k4 in range(4):
                    tb = 4 + (k4 % 2)
                    wait_free("tensor", ("bank", tb))
                    for kk in range(4):
                        k = k4 * 4 + kk
                        P.op("tensor", lambda e, k=k, kk=kk, tb=tb, ti=ti: e.transpose(
                            bank(tb)[:, kk * 128:(kk + 1) * 128],
                            xnbuf[:, ti % 2, k * 128:(k + 1) * 128], eyef[:]),
                            inc=(sT if kk == 3 else None))
                    P.wait("scalar", sT)
                    do = dst_off + ti * 128
                    v_e = P.op("scalar", lambda e, k4=k4, tb=tb, dst=dst, do=do: e.activation(
                        dst[:, k4 * 4:k4 * 4 + 4, do:do + 128],
                        bank(tb)[:].rearrange("p (a b) -> p a b", a=4), AF.Identity), inc=sV2)
                    free_set(("bank", tb), (sV2, v_e))
                free_set(("xnbuf", ti % 2), (sV2, sV2.v))
            free_set(("stat",), (sV, v_xn))

        def kv_group(src, src_off, tok0):
            for which in ("k", "v"):
                wsrc = WK if which == "k" else WV
                bsrc = bk_sb if which == "k" else bv_sb
                for m in range(4):
                    psm = bank(m)
                    wait_free("gpsimd", ("wbuf", m % 2))
                    P.op("gpsimd", lambda e, m=m, wsrc=wsrc: e.dma_start(
                        out=wbuf[:, m % 2, :, :], in_=wsrc[m]), inc=sWB[m % 2], n=16)
                    P.wait("tensor", sWB[m % 2])
                    wait_free("tensor", ("bank", m))
                    for k in range(16):
                        P.op("tensor", lambda e, m=m, k=k, psm=psm, src=src, src_off=src_off: e.matmul(
                            psm, wbuf[:, m % 2, k, :], src[:, k, src_off:src_off + 512],
                            start=(k == 0), stop=(k == 15)), inc=(sT if k == 15 else None))
                    v_mm = sT.v
                    free_set(("wbuf", m % 2), (sT, v_mm))
                    P.wait("scalar", sT, v_mm)
                    if which == "k":
                        v_ev = P.op("scalar", lambda e, m=m, psm=psm, tok0=tok0, bsrc=bsrc: e.activation(
                            kT[:, m, tok0:tok0 + 512], psm, AF.Identity, bias=bsrc[:, m:m + 1]), inc=sS)
                        free_set(("bank", m), (sS, v_ev))
                    else:
                        wait_free("scalar", ("vtmp", m % 2))
                        v_ev = P.op("scalar", lambda e, m=m, psm=psm, bsrc=bsrc: e.activation(
                            vtmp[:, m % 2, :], psm, AF.Identity, bias=bsrc[:, m:m + 1]), inc=sS)
                        free_set(("bank", m), (sS, v_ev))
                        P.wait("tensor", sS, v_ev)
                        bb = 4 + (m % 2)
                        wait_free("tensor", ("bank", bb))
                        for tb in range(4):
                            P.op("tensor", lambda e, m=m, tb=tb, bb=bb: e.transpose(
                                bank(bb)[:, tb * 128:(tb + 1) * 128],
                                vtmp[:, m % 2, tb * 128:(tb + 1) * 128], eyef[:]),
                                inc=(sT if tb == 3 else None))
                        P.wait("scalar", sT)
                        kb0 = tok0 // 128
                        v_e = P.op("scalar", lambda e, kb0=kb0, m=m, bb=bb: e.activation(
                            vaug[:, kb0:kb0 + 4, 2 * m:2 * m + 2, 0:64],
                            bank(bb)[:].rearrange("p (a c b) -> p a c b", a=4, c=2), AF.Identity), inc=sV2)
                        free_set(("bank", bb), (sV2, v_e))
                        free_set(("vtmp", m % 2), (sV2, sV2.v))

        def q_proj(qcol, ready_cnt=None, ready_v=None):
            P.wait("vector", sV, bq8_ready)
            red = []
            for t in range(16):
                wait_free("gpsimd", ("wbuf", t % 2))
                P.op("gpsimd", lambda e, t=t: e.dma_start(out=wbuf[:, t % 2, :, :], in_=WQ[t]), inc=sWB[t % 2], n=16)
                P.wait("tensor", sWB[t % 2])
                if ready_cnt is not None:
                    P.wait("tensor", ready_cnt, ready_v)
                bslot = t % 2
                pq = bank(4 + bslot)
                wait_free("tensor", ("bank", 4 + bslot))
                wait_free("tensor", ("pv", bslot))
                for k in range(16):
                    P.op("tensor", lambda e, t=t, k=k, pq=pq, qcol=qcol: e.matmul(
                        pq, wbuf[:, t % 2, k, :], xnT_h[:, k, qcol:qcol + 512],
                        start=(k == 0), stop=(k == 15)), inc=(sT if k == 15 else None))
                v_mm = sT.v
                free_set(("wbuf", t % 2), (sT, v_mm))
                P.wait("scalar", sT, v_mm)
                v_ev = P.op("scalar", lambda e, t=t, pq=pq: e.activation(
                    qT[:, t, :], pq, AF.Identity,
                    bias=bq8[:, t:t + 1], scale=0.125), inc=sS)
                free_set(("bank", 4 + bslot), (sS, v_ev))
                free_set(("pv", bslot), (sS, v_ev))
                P.wait("scalar", sS, v_ev)
                sl = t % 2
                wait_free("scalar", ("sqb", sl))
                v_sq = P.op("scalar", lambda e, t=t, sl=sl: e.activation(
                    sqb[:, sl, 0:512], qT[:, t, :], AF.Square), inc=sS)
                P.wait("tensor", sS, v_sq)
                if t == 0:
                    P.wait("tensor", sV, ohq_ready)
                    wait_free("tensor", ("bank", 7))
                P.op("tensor", lambda e, t=t, sl=sl: e.matmul(
                    pp4[0:32, :], ohq[:, t, :], sqb[:, sl, 0:512],
                    start=(t == 0), stop=(t == 15)), inc=sT)
                free_set(("sqb", sl), (sT, sT.v))
            P.wait("vector", sT, sT.v)
            v_qr = P.op("vector", lambda e: e.tensor_reduce(
                qn[0:32, 0:1], pp4[0:32, :], axis=AX, op=ALU.max), inc=sV)
            free_set(("bank", 7), (sV, v_qr))
            return sS.v

        def compute_negc(qc_idx):
            P.wait("vector", sV)
            v_p = P.op("vector", lambda e: e.tensor_tensor(
                qn[0:32, 1:2], qn[0:32, 0:1], kqb[0:32, 0:1], ALU.mult), inc=sV)
            P.wait("scalar", sV, v_p)
            v_sq = P.op("scalar", lambda e: e.activation(
                qn[0:32, 1:2], qn[0:32, 1:2], AF.Sqrt), inc=sS)
            P.wait("vector", sS, v_sq)
            v_n = P.op("vector", lambda e: e.tensor_scalar(
                qn[0:32, 1:2], qn[0:32, 1:2], -1.0, None, ALU.mult), inc=sV)
            P.wait("tensor", sV, v_n)
            wait_free("tensor", ("bank", 7))
            v_tp = P.op("tensor", lambda e: e.transpose(
                pp4[0:1, 0:32], qn[0:32, 1:2], eyef[0:32, 0:32]), inc=sT)
            P.wait("vector", sT, v_tp)
            v_cp = P.op("vector", lambda e: e.tensor_copy(tiny[0:1, 0:32], pp4[0:1, 0:32]), inc=sV)
            P.wait("tensor", sV, v_cp)
            v_bc = P.op("tensor", lambda e: e.matmul(
                pp4[:, 0:32], onesf[0:1, 0:128], tiny[0:1, 0:32],
                start=True, stop=True), inc=sT)
            P.wait("vector", sT, v_bc)
            P.wait("vector", sS2)
            v_nc = P.op("vector", lambda e: e.tensor_copy(negc[:, 0:32], pp4[:, 0:32]), inc=sV)
            free_set(("bank", 7), (sV, v_nc))
            negc_ev[0] = v_nc
            if dbg:
                P.wait("sync", sV, v_nc)
                P.op("sync", lambda e, hi=qc_idx: e.dma_start(out=DNC[hi], in_=negc[:]), inc=sOut, n=16)

        negc_ev = [0]
        P.wait("tensor", sCstG)
        P.wait("tensor", sCst)
        tile_i = [0]
        xnT_ready = {}

        def knorm_group(grp):
            nb = 6 if grp % 2 == 0 else 7
            P.wait("tensor", sV, bq8_ready)
            wait_free("tensor", ("bank", nb))
            if nb == 6:
                wait_free("tensor", ("bc",))
            for t in range(4):
                sl = (grp * 4 + t) % 2
                wait_free("scalar", ("sqb", sl))
                v_sq = P.op("scalar", lambda e, t=t, grp=grp, sl=sl: e.activation(
                    sqb[:, sl, 0:512], kT[:, t, grp * 512:grp * 512 + 512], AF.Square), inc=sS)
                P.wait("tensor", sS, v_sq)
                P.op("tensor", lambda e, t=t, sl=sl, nb=nb: e.matmul(
                    bank(nb)[0:8, :], ohk[:, t, :], sqb[:, sl, 0:512],
                    start=(t == 0), stop=(t == 3)), inc=sT)
                free_set(("sqb", sl), (sT, sT.v))
            P.wait("vector", sT, sT.v)
            v_r = P.op("vector", lambda e, nb=nb, grp=grp: e.tensor_reduce(
                kmaxg[0:8, grp:grp + 1], bank(nb)[0:8, :], axis=AX, op=ALU.max), inc=sV)
            free_set(("bank", nb), (sV, v_r))
            if nb == 6:
                free_set(("bc",), (sV, v_r))

        def do_group(grp):
            tok0 = grp * 512
            if grp < 2:
                dst, off = xnT_h, grp * 512
            else:
                dst, off = xnT_t, 0
                wait_free("vector", ("xnT_t",))
            if grp == 0:
                quad_dma(0)
            ln_quad(grp, dst, off)
            ready = sV2.v
            P.wait("tensor", sV2, ready)
            kv_group(dst, off, tok0)
            if grp < 7:
                quad_dma(grp + 1)
            ev = [(sT, sT.v)]
            if grp < 4:
                hh, c0 = grp // 2, (grp % 2) * 512
                P.wait("sync", sV2, ready)
                v_sp = P.op("sync", lambda e, hh=hh, c0=c0, dst=dst, off=off: e.dma_start(
                    out=SPXNT[hh, :, :, c0:c0 + 512], in_=dst[:, :, off:off + 512]), inc=sSp, n=16)
                ev.append((sSp, v_sp))
            if grp >= 2:
                free_set(("xnT_t",), *ev)
            xnT_ready[grp] = ready
            if grp >= 1:
                knorm_group(grp - 1)

        for grp in range(8):
            do_group(grp)
        knorm_group(7)

        P.wait("vector", sV)
        P.op("vector", lambda e: e.tensor_reduce(
            kmaxg[0:8, 8:9], kmaxg[0:8, 0:8], axis=AX, op=ALU.max), inc=sV)
        P.wait("tensor", sV)
        wait_free("tensor", ("bank", 7))
        v_selmm = P.op("tensor", lambda e: e.matmul(
            pp4[0:32, 0:1], sel[0:8, :], kmaxg[0:8, 8:9], start=True, stop=True), inc=sT)
        P.wait("vector", sT, v_selmm)
        v_kqb = P.op("vector", lambda e: e.tensor_copy(kqb[:, 0:1], pp4[0:32, 0:1]), inc=sV)
        free_set(("bank", 7), (sV, v_kqb))

        if dbg:
            P.wait("sync", sS)
            P.wait("sync", sV2)
            P.op("sync", lambda e: e.dma_start(out=DKT, in_=kT[:]), inc=sOut, n=16)
            P.op("sync", lambda e: e.dma_start(out=DVA, in_=vaug[:]), inc=sOut, n=16)

        barrier()
        es1.close()

        es3 = contextlib.ExitStack()
        sbuf3 = lambda name, shape, dt: es3.enter_context(nc.sbuf_tensor(name, shape, dt))
        qT = sbuf3("qT", [128, 16, 512], BF16)
        ohq = sbuf3("ohq", [128, 16, 32], BF16)
        ebl = sbuf3("ebl", [128, 2, NEBL, 512], BF16)
        etmp = sbuf3("etmp", [128, 4, 1024], BF16)
        ebig = sbuf3("ebig", [128, 3, 8, 512], BF16)
        rz = sbuf3("rz", [1, 2, 512], F32)
        rzbf = sbuf3("rzbf", [1, 2, 512], BF16)
        rzbs = sbuf3("rzbs", [64, 2, 512], BF16)
        aoT = sbuf3("aoT", [128, 16, 512], BF16)

        def head_of_qcol(j):
            return 4 * (2 * (j // 8) + (j % 2)) + (j % 8) // 2

        P.op("vector", lambda e: e.memset(ohq[:], 0.0), inc=sV)
        P.wait("vector", sV)
        for t in range(16):
            h0, h1 = head_of_qcol(2 * t), head_of_qcol(2 * t + 1)
            P.op("vector", lambda e, t=t, h0=h0: e.memset(ohq[0:64, t, h0:h0 + 1], 1.0), inc=sV)
            P.op("vector", lambda e, t=t, h1=h1: e.memset(ohq[64:128, t, h1:h1 + 1], 1.0), inc=sV)
        ohq_ready = sV.v

        pvp = [pp[2][0:65, i, :] for i in range(2)]
        pvz = [pp[2][64:65, i, :] for i in range(2)]
        pvo = [pp[2][0:64, i, :] for i in range(2)]
        bcp = pp3[0:64, :]

        def pair_entries(qcbase):
            ents = sorted(kb_entries(qcbase), key=lambda x: x[1])
            out, i = [], 0
            while i < len(ents):
                if i + 1 < len(ents) and ents[i + 1][1] == ents[i][1] + 1:
                    out.append((ents[i], ents[i + 1]))
                    i += 2
                else:
                    out.append((ents[i], None))
                    i += 1
            return out

        attn_hi = [0]

        def attn_all():
            items = [(qc, (qc % 2) * 512, g) for qc in range(4) for g in range(NG)]
            ld = {}

            def load_item(j):
                slot = j % 2
                g = items[j][2]
                wait_free("gpsimd", ("ebl", slot))
                P.op("gpsimd", lambda e, g=g, slot=slot: e.dma_start(
                    out=ebl[:, slot, :, :], in_=EBL[g]), inc=sEB[slot], n=16)
                ld[j] = (sEB[slot], sEB[slot].v)

            load_item(0)
            P.wait("vector", sSp)
            for j, (qi, qcb, g) in enumerate(items):
                if j + 1 < len(items):
                    load_item(j + 1)
                if g == 0:
                    if dbg:
                        P.wait("scalar", sOut)
                    if qi == 2:
                        P.wait("sync", sT)
                        P.wait("sync", sSp)
                        P.op("sync", lambda e: e.dma_start(out=xnT_h[:], in_=SPXNT[1]), inc=sXH, n=16)
                    if qi >= 2:
                        q_proj((qi % 2) * 512, sXH, sXH.v)
                    else:
                        q_proj((qi % 2) * 512)
                    if dbg:
                        P.wait("sync", sS)
                        P.op("sync", lambda e, qi=qi: e.dma_start(out=DQT[qi], in_=qT[:]), inc=sOut, n=16)
                    compute_negc(qi)
                    wait_free("vector", ("aoT",))
                slot_e = j % 2
                prs = pair_entries(qi * 512)
                npr = len(prs)
                P.wait("tensor", ld[j][0], ld[j][1])
                P.wait("gpsimd", ld[j][0], ld[j][1])
                P.wait("vector", ld[j][0], ld[j][1])

                pair_ctr = [0]
                scinfo = {}

                def scpart(kv):
                    h = kv * NG + g
                    tq, base = q_tile_of_head(h)
                    ktile, kbase = kv // 2, 64 * (kv % 2)
                    info = []
                    for pi, ((kb_a, tix_a), eb) in enumerate(prs):
                        ss = pair_ctr[0] % 2
                        es4 = pair_ctr[0] % 4
                        pair_ctr[0] += 1
                        width = 1024 if eb else 512
                        wait_free("tensor", ("scp", ss))
                        if eb:
                            P.op("tensor", lambda e, kb=kb_a, ss=ss, ktile=ktile, kbase=kbase,
                                 tq=tq, base=base: e.matmul(
                                pp[ss][:, 0, :], kT[kbase:kbase + 64, ktile, kb * 128:(kb + 1) * 128],
                                qT[base:base + 64, tq, :], start=True, stop=True))
                            v_sc = P.op("tensor", lambda e, kb=eb[0], ss=ss, ktile=ktile, kbase=kbase,
                                        tq=tq, base=base: e.matmul(
                                pp[ss][:, 1, :], kT[kbase:kbase + 64, ktile, kb * 128:(kb + 1) * 128],
                                qT[base:base + 64, tq, :], start=True, stop=True), inc=sT)
                        else:
                            v_sc = P.op("tensor", lambda e, kb=kb_a, ss=ss, ktile=ktile, kbase=kbase,
                                        tq=tq, base=base: e.matmul(
                                pp[ss][:, 0, :], kT[kbase:kbase + 64, ktile, kb * 128:(kb + 1) * 128],
                                qT[base:base + 64, tq, :], start=True, stop=True), inc=sT)
                        P.wait("scalar", sT, v_sc)
                        P.wait("scalar", sV, negc_ev[0])
                        wait_free("scalar", ("etmp", es4))
                        v_ex = P.op("scalar", lambda e, es4=es4, ss=ss, w=width, h=h: e.activation(
                            etmp[:, es4, 0:w],
                            pp[ss][:].rearrange("p a b -> p (a b)")[:, 0:w],
                            AF.Exp, bias=negc[:, h:h + 1]), inc=sS2)
                        free_set(("scp", ss), (sS2, v_ex))
                        info.append((pi, es4, width, tix_a, eb, v_ex))
                    scinfo[kv] = info

                def multpart(kv):
                    eslot = kv % 3
                    mu_ev = []
                    for (pi, es4, width, tix_a, eb, v_ex) in scinfo.pop(kv):
                        meng, mcnt = ("gpsimd", sG) if (pi % 2 == 0) else ("vector", sV2)
                        P.wait(meng, sS2, v_ex)
                        wait_free(meng, ("ebig", eslot))
                        v_mu = P.op(meng, lambda e, es4=es4, w=width, tix=tix_a, slot_e=slot_e,
                                    eslot=eslot, pi=pi, nt=(2 if eb else 1): e.tensor_tensor(
                            ebig[:, eslot, 2 * pi:2 * pi + nt, :].rearrange("p a b -> p (a b)"),
                            etmp[:, es4, 0:w],
                            ebl[:, slot_e, tix:tix + nt, :].rearrange("p a b -> p (a b)"),
                            ALU.mult), inc=mcnt)
                        free_set(("etmp", es4), (mcnt, v_mu))
                        mu_ev.append((mcnt, v_mu))
                    return mu_ev

                def pvpart(kv, mu_ev):
                    h = kv * NG + g
                    eslot = kv % 3
                    pvs = attn_hi[0] % 2
                    for (cnt, v) in mu_ev:
                        P.wait("tensor", cnt, v)
                    wait_free("tensor", ("pv", pvs))
                    wait_free("tensor", ("bank", 4 + pvs))
                    kbs = []
                    for (kb_a, tix_a), eb in prs:
                        kbs.append(kb_a)
                        if eb:
                            kbs.append(eb[0])
                    nk = len(kbs)
                    for ki, kb in enumerate(kbs):
                        P.op("tensor", lambda e, kb=kb, kv=kv, ki=ki, eslot=eslot, pvs=pvs,
                             first=(ki == 0), last=(ki == nk - 1): e.matmul(
                            pvp[pvs], vaug[:, kb, kv, :], ebig[:, eslot, ki, :],
                            start=first, stop=last), inc=(sT2 if ki == nk - 1 else None))
                    free_set(("ebig", eslot), (sT2, sT2.v))
                    v_pvlast = sT2.v
                    P.wait("scalar", sT2, v_pvlast)
                    wait_free("scalar", ("rz", pvs))
                    P.op("scalar", lambda e, pvs=pvs: e.activation(
                        rz[0:1, pvs, :], pvz[pvs], AF.Ln), inc=sS2)
                    P.wait("scalar", sS2)
                    wait_free("scalar", ("rzbf", pvs))
                    v_rzf = P.op("scalar", lambda e, pvs=pvs: e.activation(
                        rzbf[0:1, pvs, :], rz[0:1, pvs, :], AF.Exp, scale=-1.0), inc=sS2)
                    free_set(("rz", pvs), (sS2, v_rzf))
                    pend.append((h, pvs, v_rzf))
                    attn_hi[0] += 1

                pend = []

                def flush_epi():
                    (h_p, pvs_p, v_rzf_p) = pend.pop(0)
                    P.wait("tensor", sS2, v_rzf_p)
                    wait_free("tensor", ("bc",))
                    wait_free("tensor", ("bank", 6))
                    v_bc = P.op("tensor", lambda e, pvs=pvs_p: e.matmul(
                        bcp, ones_b16[0:1, 0:64], rzbf[0:1, pvs, :], start=True, stop=True), inc=sT)
                    free_set(("rzbf", pvs_p), (sT, v_bc))
                    P.wait("vector", sT, v_bc)
                    wait_free("vector", ("rzbs", pvs_p))
                    v_cp = P.op("vector", lambda e, pvs=pvs_p: e.tensor_copy(rzbs[:, pvs, :], bcp), inc=sV)
                    free_set(("bc",), (sV, v_cp))
                    free_set(("bank", 6), (sV, v_cp))
                    P.wait("vector", sV, v_cp)
                    ta, abase = h_p // 2, 64 * (h_p % 2)
                    v_ao = P.op("vector", lambda e, ta=ta, abase=abase, pvs=pvs_p: e.tensor_tensor(
                        aoT[abase:abase + 64, ta, :], pvo[pvs], rzbs[:, pvs, :], ALU.mult), inc=sV)
                    free_set(("pv", pvs_p), (sV, v_ao))
                    free_set(("bank", 4 + pvs_p), (sV, v_ao))
                    free_set(("rzbs", pvs_p), (sV, v_ao))

                scpart(0); mu0 = multpart(0)
                scpart(1); mu1 = multpart(1)
                mu = {0: mu0, 1: mu1}
                for kv in (0, 2, 4, 6):
                    if kv + 2 < 8:
                        scpart(kv + 2)
                        mu[kv + 2] = multpart(kv + 2)
                    if kv + 3 < 8:
                        scpart(kv + 3)
                    pvpart(kv, mu.pop(kv))
                    if kv + 3 < 8:
                        mu[kv + 3] = multpart(kv + 3)
                    pvpart(kv + 1, mu.pop(kv + 1))
                    while pend:
                        flush_epi()
                while pend:
                    flush_epi()
                free_set(("ebl", slot_e), (sG, sG.v), (sV2, sV2.v))
                if g == NG - 1:
                    P.wait("sync", sV)
                    v_sp = P.op("sync", lambda e, qi=qi: e.dma_start(out=SPAOT[qi], in_=aoT[:]), inc=sSp, n=16)
                    free_set(("aoT",), (sSp, v_sp))

        attn_all()
        barrier()

        es3.close()
        es2.close()
        es4 = contextlib.ExitStack()
        sbuf4 = lambda name, shape, dt: es4.enter_context(nc.sbuf_tensor(name, shape, dt))
        woT = sbuf4("woT", [128, 16, D], BF16)
        aom = sbuf4("aom", [128, 2, 16, 512], BF16)
        sm = sbuf4("sm", [128, 2, D], BF16)

        P.wait("gpsimd", sSp)
        if dbg:
            P.wait("gpsimd", sOut)
            P.wait("sync", sOut)
        for k in range(16):
            P.op("gpsimd", lambda e, k=k: e.dma_start(out=woT[:, k, :], in_=WO[k]), inc=sCstG, n=16)
        P.wait("tensor", sCstG)
        P.wait("sync", sSp)
        P.op("sync", lambda e: e.dma_start(out=aom[:, 0, :, :], in_=SPAOT[0]), inc=sAM[0], n=16)
        for qi in range(4):
            if qi + 1 < 4:
                wait_free("sync", ("aom", (qi + 1) % 2))
                P.op("sync", lambda e, qi=qi: e.dma_start(
                    out=aom[:, (qi + 1) % 2, :, :], in_=SPAOT[qi + 1]), inc=sAM[(qi + 1) % 2], n=16)
            P.wait("tensor", sAM[qi % 2])
            for mm_ in range(4):
                m = qi * 4 + mm_
                for n in range(4):
                    pso = bank(n)
                    wait_free("tensor", ("bank", n))
                    for k in range(16):
                        P.op("tensor", lambda e, qi=qi, mm_=mm_, n=n, k=k, pso=pso: e.matmul(
                            pso, aom[:, qi % 2, k, mm_ * 128:(mm_ + 1) * 128],
                            woT[:, k, n * 512:(n + 1) * 512],
                            start=(k == 0), stop=(k == 15)), inc=(sT if k == 15 else None))
                    v_mm = sT.v
                    P.wait("scalar", sT, v_mm)
                    if n == 0:
                        wait_free("scalar", ("sm", m % 2))
                    v_ev = P.op("scalar", lambda e, m=m, n=n, pso=pso: e.activation(
                        sm[:, m % 2, n * 512:(n + 1) * 512], pso, AF.Identity), inc=sS)
                    free_set(("bank", n), (sS, v_ev))
                if mm_ == 3:
                    free_set(("aom", qi % 2), (sT, sT.v))
                P.wait("sync", sS)
                v_sp = P.op("sync", lambda e, m=m: e.dma_start(out=SPS[m], in_=sm[:, m % 2, :]), inc=sSm[m % 2], n=16)
                free_set(("sm", m % 2), (sSm[m % 2], v_sp))

        barrier()
        bank8 = bank

        def pair(i):
            return pp[i][:].rearrange("p a b -> p (a b)")

        es4.close()
        P.wait("gpsimd", sSm[0])
        P.wait("gpsimd", sSm[1])
        P.wait("sync", sSm[0])
        P.wait("sync", sSm[1])
        rtr = sbuf("rtr", [128, 16, 8], BF16)
        lgf = sbuf("lgf", [8, 1024], F32)
        logT = sbuf("logT", [128, 8, 8], F32)
        idx8 = sbuf("idx8", [128, 8], F32)
        idxB = sbuf("idxB", [128, 8], F32)
        msk = sbuf("msk", [128, 8], F32)
        m1k = sbuf("m1k", [128, 8], F32)
        m2k = sbuf("m2k", [128, 8], F32)
        tmp8 = sbuf("tmp8", [128, 8], F32)
        r6 = sbuf("r6", [128, 6], F32)
        combT = sbuf("combT", [128, 8, 8], F32)
        gupw = sbuf("gupw", [128, 2, 16, 256], BF16)
        dwnw = sbuf("dwnw", [128, 2, 11, 512], BF16)
        he = sbuf("he", [128, 11, HALF], BF16)
        silu_t = sbuf("silu_t", [128, 2, 1024], BF16)
        acc = sbuf("acc", [128, 8, D], F32)
        gbuf = sbuf("gbuf", [128, D], F32)
        sob = sbuf("sob", [128, 2, D], BF16)
        sobf = sbuf("sobf", [128, D], F32)
        outb = sbuf("outb", [128, 2, D], F32)

        P.op("gpsimd", lambda e: e.dma_start(out=rtr[:], in_=RTR), inc=sCstG, n=16)
        v_rtr = sCstG.v
        v_iota = P.op("gpsimd", lambda e: e.iota(idx8[:], pattern=[[1, 8]], base=0, channel_multiplier=0,
                                                 allow_small_or_imprecise_dtypes=True), inc=sG)
        BIGC = 65536.0
        P.wait("vector", sG, v_iota)
        P.op("vector", lambda e: e.tensor_scalar(idxB[:], idx8[:], BIGC, None, ALU.add), inc=sV)

        def vop(fn):
            P.op("vector", fn, inc=sV)
            P.wait("vector", sV)

        def router(hh):
            if dbg:
                P.wait("vector", sOut)
            P.wait("tensor", sCstG, v_rtr)
            for n in range(2):
                wait_free("tensor", ("dbank", 4))
                for k in range(16):
                    P.op("tensor", lambda e, n=n, k=k: e.matmul(
                        bank8(4)[0:8, :], rtr[:, k, :], xnT_h[:, k, n * 512:(n + 1) * 512],
                        start=(k == 0), stop=(k == 15)), inc=(sT if k == 15 else None))
                P.wait("vector", sT)
                v_ev = P.op("vector", lambda e, n=n: e.tensor_copy(
                    lgf[0:8, n * 512:(n + 1) * 512], bank8(4)[0:8, :]), inc=sV)
                free_set(("dbank", 4), (sV, v_ev))
            P.wait("tensor", sV)
            for tb in range(8):
                wait_free("tensor", ("dbank", 5))
                v_t = P.op("tensor", lambda e, tb=tb: e.transpose(
                    bank8(5)[0:128, 0:8], lgf[0:8, tb * 128:(tb + 1) * 128], eyef[0:8, 0:8]), inc=sT)
                P.wait("vector", sT, v_t)
                v_e = P.op("vector", lambda e, tb=tb: e.tensor_copy(logT[:, tb, :], bank8(5)[0:128, 0:8]), inc=sV)
                free_set(("dbank", 5), (sV, v_e))
            for tb in range(8):
                L = logT[:, tb, :]
                vop(lambda e, L=L: e.tensor_reduce(r6[:, 0:1], L, axis=AX, op=ALU.max))
                vop(lambda e, L=L: e.tensor_scalar(msk[:], L, r6[:, 0:1], None, ALU.is_equal))
                vop(lambda e: e.scalar_tensor_tensor(tmp8[:], msk[:], -BIGC, idxB[:], ALU.mult, ALU.add))
                vop(lambda e: e.tensor_reduce(r6[:, 1:2], tmp8[:], axis=AX, op=ALU.min))
                vop(lambda e: e.tensor_scalar(m1k[:], idx8[:], r6[:, 1:2], None, ALU.is_equal))
                vop(lambda e, L=L: e.scalar_tensor_tensor(tmp8[:], m1k[:], -BIGC, L, ALU.mult, ALU.add))
                vop(lambda e: e.tensor_reduce(r6[:, 2:3], tmp8[:], axis=AX, op=ALU.max))
                vop(lambda e: e.tensor_scalar(msk[:], tmp8[:], r6[:, 2:3], None, ALU.is_equal))
                vop(lambda e: e.scalar_tensor_tensor(tmp8[:], msk[:], -BIGC, idxB[:], ALU.mult, ALU.add))
                vop(lambda e: e.tensor_reduce(r6[:, 3:4], tmp8[:], axis=AX, op=ALU.min))
                vop(lambda e: e.tensor_scalar(m2k[:], idx8[:], r6[:, 3:4], None, ALU.is_equal))
                vop(lambda e: e.tensor_tensor(r6[:, 4:5], r6[:, 0:1], r6[:, 2:3], ALU.subtract))
                P.wait("scalar", sV)
                P.op("scalar", lambda e: e.activation(r6[:, 4:5], r6[:, 4:5], AF.Sigmoid), inc=sS)
                P.wait("vector", sS)
                vop(lambda e: e.tensor_scalar(r6[:, 5:6], r6[:, 4:5], -1.0, 1.0, ALU.mult, ALU.add))
                vop(lambda e: e.tensor_scalar(m1k[:], m1k[:], r6[:, 4:5], None, ALU.mult))
                vop(lambda e: e.tensor_scalar(m2k[:], m2k[:], r6[:, 5:6], None, ALU.mult))
                vop(lambda e, tb=tb: e.tensor_tensor(combT[:, tb, :], m1k[:], m2k[:], ALU.add))
            if dbg:
                P.wait("sync", sV)
                P.op("sync", lambda e, hh=hh: e.dma_start(out=DCB[hh], in_=combT[:]), inc=sOut, n=16)

        def moe_experts(hh, interleave=None):
            for e_ in range(NE):
                for hg in range(11):
                    if interleave:
                        interleave.pop(0)()
                    sl = hg % 2
                    wait_free("gpsimd", ("gupw", sl))
                    P.op("gpsimd", lambda e, e_=e_, hg=hg, sl=sl: e.dma_start(
                        out=gupw[:, sl, :, :], in_=GUP[e_, hg]), inc=sGW[sl], n=16)
                    P.wait("tensor", sGW[sl])
                    gb = 0 if hg % 2 == 0 else 1
                    ub0, ub1 = (4, 5) if hg % 2 == 0 else (6, 7)
                    for db in (2 * gb, 2 * gb + 1, ub0, ub1):
                        wait_free("tensor", ("dbank", db))
                    for k in range(16):
                        P.op("tensor", lambda e, sl=sl, k=k, gb=gb: e.matmul(
                            pp[gb][:, 0, :], gupw[:, sl, k, 0:128], xnT_h[:, k, 0:512],
                            start=(k == 0), stop=(k == 15)))
                        P.op("tensor", lambda e, sl=sl, k=k, gb=gb: e.matmul(
                            pp[gb][:, 1, :], gupw[:, sl, k, 0:128], xnT_h[:, k, 512:1024],
                            start=(k == 0), stop=(k == 15)))
                        P.op("tensor", lambda e, sl=sl, k=k, ub0=ub0: e.matmul(
                            bank8(ub0), gupw[:, sl, k, 128:256], xnT_h[:, k, 0:512],
                            start=(k == 0), stop=(k == 15)))
                        P.op("tensor", lambda e, sl=sl, k=k, ub1=ub1: e.matmul(
                            bank8(ub1), gupw[:, sl, k, 128:256], xnT_h[:, k, 512:1024],
                            start=(k == 0), stop=(k == 15)), inc=(sT if k == 15 else None))
                    v_mm = sT.v
                    free_set(("gupw", sl), (sT, v_mm))
                    P.wait("scalar", sT, v_mm)
                    wait_free("scalar", ("silu", sl))
                    v_si = P.op("scalar", lambda e, sl=sl, gb=gb: e.activation(
                        silu_t[:, sl, :], pair(gb), AF.Silu), inc=sS)
                    free_set(("dbank", 2 * gb), (sS, v_si))
                    free_set(("dbank", 2 * gb + 1), (sS, v_si))
                    P.wait("vector", sS, v_si)
                    wait_free("vector", ("he", hg))
                    if hg % 2 == 0:
                        v_he = P.op("vector", lambda e, sl=sl, hg=hg: e.tensor_tensor(
                            he[:, hg, :], silu_t[:, sl, :], pair(2), ALU.mult), inc=sV2)
                        free_set(("dbank", 4), (sV2, v_he))
                        free_set(("dbank", 5), (sV2, v_he))
                    else:
                        P.op("vector", lambda e, sl=sl, hg=hg: e.tensor_tensor(
                            he[:, hg, 0:512], silu_t[:, sl, 0:512], bank8(6), ALU.mult), inc=sV2)
                        v_he = P.op("vector", lambda e, sl=sl, hg=hg: e.tensor_tensor(
                            he[:, hg, 512:1024], silu_t[:, sl, 512:1024], bank8(7), ALU.mult), inc=sV2)
                        free_set(("dbank", 6), (sV2, v_he))
                        free_set(("dbank", 7), (sV2, v_he))
                    free_set(("silu", sl), (sV2, v_he))
                v_he_all = sV2.v
                if e_ == NE - 1:
                    free_set(("xnT_h",), (sT, sT.v))
                P.wait("tensor", sV2, v_he_all)
                for n in range(4):
                    sl = n % 2
                    wait_free("gpsimd", ("dwnw", sl))
                    P.op("gpsimd", lambda e, e_=e_, n=n, sl=sl: e.dma_start(
                        out=dwnw[:, sl, :, :], in_=DWN[e_, n]), inc=sDW[sl], n=16)
                    P.wait("tensor", sDW[sl])
                    for mh in range(2):
                        ms = range(4 * mh, 4 * mh + 4)
                        for m in ms:
                            wait_free("tensor", ("dbank", m))
                        for k in range(11):
                            for m in ms:
                                P.op("tensor", lambda e, sl=sl, k=k, m=m: e.matmul(
                                    bank8(m), he[:, k, m * 128:(m + 1) * 128], dwnw[:, sl, k, :],
                                    start=(k == 0), stop=(k == 10)),
                                    inc=(sT if (k == 10 and m == 4 * mh + 3) else None))
                        v_mm = sT.v
                        if mh == 1:
                            free_set(("dwnw", sl), (sT, v_mm))
                        P.wait("vector", sT, v_mm)
                        for m in ms:
                            cap = combT[:, m, e_:e_ + 1]
                            asl = acc[:, m, n * 512:(n + 1) * 512]
                            if e_ == 0:
                                wait_free("vector", ("accm", m))
                                v_up = P.op("vector", lambda e, m=m, cap=cap, asl=asl: e.tensor_scalar(
                                    asl, bank8(m), cap, None, ALU.mult), inc=sV)
                            else:
                                v_up = P.op("vector", lambda e, m=m, cap=cap, asl=asl: e.scalar_tensor_tensor(
                                    asl, bank8(m), cap, asl, ALU.mult, ALU.add), inc=sV)
                            free_set(("dbank", m), (sV, v_up))
                for hg in range(11):
                    free_set(("he", hg), (sT, sT.v))
        def moe_combine_units(hh):
            units = []
            v_acc_done = sV.v
            for m in range(8):
                units.append(lambda m=m, hh=hh, v_acc_done=v_acc_done: _combine_m(hh, m, v_acc_done))
            return units

        def _combine_m(hh, m, v_acc_done):
            if True:
                P.wait("scalar", sV, v_acc_done)
                wait_free("scalar", ("gbuf",))
                v_g = P.op("scalar", lambda e, m=m: e.activation(gbuf[:], acc[:, m, :], AF.Silu), inc=sS)
                free_set(("accm", m), (sS, v_g))
                rows = hh * HALF + m * 128
                wait_free("sync", ("outb", m % 2))
                P.op("sync", lambda e, rows=rows, m=m: e.dma_start(
                    out=outb[:, m % 2, :], in_=X[rows:rows + 128, :]), inc=sXL[m % 2], n=16)
                wait_free("sync", ("sob", m % 2))
                P.wait("sync", sSm[0])
                P.wait("sync", sSm[1])
                P.op("sync", lambda e, hh=hh, m=m: e.dma_start(
                    out=sob[:, m % 2, :], in_=SPS[hh * 8 + m]), inc=sSB[m % 2], n=16)
                P.wait("vector", sXL[m % 2])
                P.wait("vector", sSB[m % 2])
                P.wait("vector", sS, v_g)
                wait_free("vector", ("sobf",))
                v_sf = P.op("vector", lambda e, m=m: e.tensor_copy(sobf[:], sob[:, m % 2, :]), inc=sV)
                free_set(("sob", m % 2), (sV, v_sf))
                P.wait("vector", sV, v_sf)
                v_o1 = P.op("vector", lambda e, m=m: e.tensor_tensor(
                    gbuf[:], sobf[:], gbuf[:], ALU.mult), inc=sV)
                free_set(("sobf",), (sV, v_o1))
                P.wait("vector", sV, v_o1)
                v_o2 = P.op("vector", lambda e, m=m: e.tensor_tensor(
                    outb[:, m % 2, :], outb[:, m % 2, :], gbuf[:], ALU.add), inc=sV)
                free_set(("gbuf",), (sV, v_o2))
                P.wait("sync", sV, v_o2)
                v_st = P.op("sync", lambda e, rows=rows, m=m: e.dma_start(
                    out=OUT[rows:rows + 128, :], in_=outb[:, m % 2, :]), inc=sO[m % 2], n=16)
                free_set(("outb", m % 2), (sO[m % 2], v_st))

        wait_free("sync", ("xnT_h",))
        P.wait("sync", sSp)
        P.op("sync", lambda e: e.dma_start(out=xnT_h[:], in_=SPXNT[0]), inc=sXH, n=16)
        P.wait("tensor", sXH)
        router(0)
        moe_experts(0)
        wait_free("sync", ("xnT_h",))
        P.op("sync", lambda e: e.dma_start(out=xnT_h[:], in_=SPXNT[1]), inc=sXH, n=16)
        P.wait("tensor", sXH)
        P.wait("vector", sV)
        router(1)
        moe_experts(1, interleave=moe_combine_units(0))
        for u in moe_combine_units(1):
            u()

        P.wait("sync", sO[0])
        P.wait("sync", sO[1])
        P.wait("sync", sSp)
        P.wait("sync", sSm[0])
        P.wait("sync", sSm[1])
        if dbg:
            P.wait("sync", sOut)

        block = es.enter_context(nc.Block())
        for eng_name in Plan.ENGS:
            fns = P.streams[eng_name]
            getattr(block, eng_name)(lambda e, fns=fns: [fn(e) for fn in fns])

    return nc



_CACHE = {}


def _packs(inp):
    import ml_dtypes
    f32 = np.float32
    Wq = np.asarray(inp["L0_attn_Wq"], f32)
    Wk = np.asarray(inp["L0_attn_Wk"], f32)
    Wv = np.asarray(inp["L0_attn_Wv"], f32)
    Wo = np.asarray(inp["L0_attn_Wo"], f32)
    bq = np.asarray(inp["L0_attn_bq"], f32)
    bk = np.asarray(inp["L0_attn_bk"], f32)
    bv = np.asarray(inp["L0_attn_bv"], f32)
    gup = np.asarray(inp["L0_ffn_gate_up_proj"], f32)
    dwn = np.asarray(inp["L0_ffn_down_proj"], f32)
    rtr = np.asarray(inp["L0_ffn_router_w"], f32)

    WQp = np.empty((16, 128, 16, 128), f32)
    BQp = np.empty((16, 128), f32)
    WqT = Wq.reshape(2048, 16, 128).transpose(1, 2, 0)
    for t in range(16):
        A = 8 * (t // 4) + (t % 4)
        feats = np.concatenate([A * 64 + np.arange(64), (A + 4) * 64 + np.arange(64)])
        WQp[t] = WqT[:, :, feats].transpose(1, 0, 2)
        BQp[t] = bq[feats]
    WKp = np.ascontiguousarray(Wk.reshape(4, 128, 16, 128).transpose(0, 3, 2, 1))
    WVp = np.ascontiguousarray(Wv.reshape(4, 128, 16, 128).transpose(0, 3, 2, 1))
    BKp = np.ascontiguousarray(bk.reshape(4, 128))
    BVp = np.ascontiguousarray(bv.reshape(4, 128))
    WOp = np.ascontiguousarray(Wo.T.reshape(16, 128, 2048))

    gate = gup[:, :NH, :].reshape(NE, 11, 128, 16, 128)
    up = gup[:, NH:, :].reshape(NE, 11, 128, 16, 128)
    GUPp = np.ascontiguousarray(
        np.concatenate([gate.transpose(0, 1, 4, 3, 2), up.transpose(0, 1, 4, 3, 2)], axis=4))
    DWNp = np.ascontiguousarray(
        dwn.transpose(0, 2, 1).reshape(NE, 11, 128, 4, 512).transpose(0, 3, 2, 1, 4))
    RTRp = np.ascontiguousarray(rtr.T.reshape(16, 128, 8).transpose(1, 0, 2))
    EYEp = np.eye(128, dtype=f32)
    SELp = np.zeros((8, 32), f32)
    SELp[np.arange(32) // 4, np.arange(32)] = 1.0

    i_ = np.arange(512)[None, :]
    p_ = np.arange(128)[:, None]
    ebls = []
    for f in (0, 1):
        tbl = np.zeros((NG, 128, NEBL, 512), np.float32)
        for ti in range(NEBL):
            if ti < len(OWN_SIGMAS):
                sig, wrap = OWN_SIGMAS[ti], 0
            else:
                sig, wrap = OTHER_SIGMAS[ti - len(OWN_SIGMAS)], 4096 * f
            dist = np.abs(i_ - p_ + sig + wrap).astype(np.float32)
            for g in range(NG):
                tbl[g, :, ti, :] = np.exp(-SLOPES[g] * dist)
        ebls.append(tbl)

    shared = dict(WQ=WQp, WK=WKp, WV=WVp, WO=WOp, BQ=BQp, BK=BKp, BV=BVp,
                  GUP=GUPp, DWN=DWNp, RTR=RTRp, EYE=EYEp, SEL=SELp)
    return shared, ebls


def make_in_maps(inp):
    x = np.asarray(inp["L0_attn_hidden_states"], np.float32)
    shared, ebls = _packs(inp)
    in_maps = []
    for c in range(8):
        b, h = c // 2, c % 2
        Xc = x[b] if h == 0 else np.ascontiguousarray(np.roll(x[b], -OWN, axis=0))
        m = dict(shared)
        m["X"] = Xc
        m["EBL"] = ebls[h]
        in_maps.append(m)
    return in_maps


def kernel(**inp):
    if "nc" not in _CACHE:
        _CACHE["nc"] = build_program(dbg=False)
    nc = _CACHE["nc"]
    in_maps = make_in_maps(inp)
    res = run_bass_kernel_spmd(nc, in_maps, core_ids=list(range(8)))
    out = np.empty((NB, S, D), np.float32)
    for c in range(8):
        b, h = c // 2, c % 2
        out[b, h * OWN:(h + 1) * OWN] = res.results[c]["OUT"]
    return out


# version 20
# speedup vs baseline: 1.0480x; 1.0350x over previous
import contextlib
import numpy as np

import concourse.bass as bass
import concourse.mybir as mybir
from concourse.bass_utils import run_bass_kernel_spmd

F32 = mybir.dt.float32
BF16 = mybir.dt.bfloat16
AF = mybir.ActivationFunctionType
ALU = mybir.AluOpType
AX = mybir.AxisListType.X

D = 2048
S = 4096
NB = 4
OWN = 2048
HALF = 1024
NKV = 8
NQ = 32
NG = 4
NE = 8
NH = 1408
EPS = 1e-5
SLOPES = [2.0 ** (-g / NG) for g in range(NG)]

OWN_SIGMAS = list(range(-640, 257, 128))
OTHER_SIGMAS = [-640, -512, -3968, -3840]
NEBL = len(OWN_SIGMAS) + len(OTHER_SIGMAS)


def kb_entries(qcbase):
    out = []
    for kb in range(32):
        sig = qcbase - kb * 128
        if kb < 16 and sig in OWN_SIGMAS:
            out.append((kb, OWN_SIGMAS.index(sig)))
        elif kb >= 16 and sig in OTHER_SIGMAS:
            out.append((kb, len(OWN_SIGMAS) + OTHER_SIGMAS.index(sig)))
    return out


def q_tile_of_head(h):
    kv, g = h // NG, h % NG
    return (kv // 2) * NG + g, 64 * (kv % 2)


class Cnt:
    def __init__(self, es, nc, name):
        self.h = es.enter_context(nc.semaphore(name))
        self.v = 0

    def bump(self, n=1):
        self.v += n
        return self.v


class Plan:
    ENGS = ("sync", "tensor", "scalar", "vector", "gpsimd")

    def __init__(self):
        self.streams = {e: [] for e in self.ENGS}
        self.seen = {e: {} for e in self.ENGS}

    def wait(self, eng, cnt, v=None):
        val = cnt.v if v is None else v
        if val <= 0:
            return
        if self.seen[eng].get(id(cnt), 0) >= val:
            return
        self.seen[eng][id(cnt)] = val
        self.streams[eng].append(lambda e, h=cnt.h, val=val: e.wait_ge(h, val))

    def op(self, eng, fn, inc=None, n=1):
        if inc is None:
            self.streams[eng].append(fn)
            return None
        v = inc.bump(n)
        self.streams[eng].append(lambda e, fn=fn, h=inc.h, n=n: fn(e).then_inc(h, n))
        return v


def build_program(dbg=False):
    nc = bass.Bass("TRN2", target_bir_lowering=False)

    X = nc.dram_tensor("X", [S, D], F32, kind="ExternalInput").ap()
    WQ = nc.dram_tensor("WQ", [16, 128, 16, 128], F32, kind="ExternalInput").ap()
    WK = nc.dram_tensor("WK", [4, 128, 16, 128], F32, kind="ExternalInput").ap()
    WV = nc.dram_tensor("WV", [4, 128, 16, 128], F32, kind="ExternalInput").ap()
    WO = nc.dram_tensor("WO", [16, 128, D], F32, kind="ExternalInput").ap()
    BQ = nc.dram_tensor("BQ", [16, 128], F32, kind="ExternalInput").ap()
    BK = nc.dram_tensor("BK", [4, 128], F32, kind="ExternalInput").ap()
    BV = nc.dram_tensor("BV", [4, 128], F32, kind="ExternalInput").ap()
    GUP = nc.dram_tensor("GUP", [NE, 11, 128, 16, 256], F32, kind="ExternalInput").ap()
    DWN = nc.dram_tensor("DWN", [NE, 4, 128, 11, 512], F32, kind="ExternalInput").ap()
    RTR = nc.dram_tensor("RTR", [128, 16, 8], F32, kind="ExternalInput").ap()
    EYE = nc.dram_tensor("EYE", [128, 128], F32, kind="ExternalInput").ap()
    SEL = nc.dram_tensor("SEL", [8, 32], F32, kind="ExternalInput").ap()
    EBL = nc.dram_tensor("EBL", [NG, 128, NEBL, 512], F32, kind="ExternalInput").ap()

    OUT = nc.dram_tensor("OUT", [OWN, D], F32, kind="ExternalOutput").ap()

    spk = "ExternalOutput" if dbg else "Internal"
    SPXNT = nc.dram_tensor("SPXNT", [2, 128, 16, HALF], BF16, kind=spk).ap()
    SPAOT = nc.dram_tensor("SPAOT", [4, 128, 16, 512], BF16, kind=spk).ap()
    SPS = nc.dram_tensor("SPS", [16, 128, D], BF16, kind=spk).ap()
    if dbg:
        DKT = nc.dram_tensor("DKT", [128, 4, S], BF16, kind="ExternalOutput").ap()
        DVA = nc.dram_tensor("DVA", [128, 32, 8, 65], BF16, kind="ExternalOutput").ap()
        DQT = nc.dram_tensor("DQT", [4, 128, 16, 512], BF16, kind="ExternalOutput").ap()
        DNC = nc.dram_tensor("DNC", [4, 128, 32], F32, kind="ExternalOutput").ap()
        DCB = nc.dram_tensor("DCB", [2, 128, 8, 8], F32, kind="ExternalOutput").ap()

    P = Plan()
    es = contextlib.ExitStack()
    free = {}

    def free_set(key, *events):
        free[key] = list(events)

    def wait_free(eng, key):
        for (cnt, v) in free.get(key, ()):
            P.wait(eng, cnt, v)

    with es:
        sbuf = lambda name, shape, dt: es.enter_context(nc.sbuf_tensor(name, shape, dt))

        eye = sbuf("eye", [128, 128], BF16)
        eyef = sbuf("eyef", [128, 128], F32)
        onesf = sbuf("onesf", [128, 128], F32)
        ones_b16 = sbuf("ones_b16", [128, 64], BF16)
        bq8 = sbuf("bq8", [128, 16], F32)
        bk_sb = sbuf("bk_sb", [128, 4], F32)
        bv_sb = sbuf("bv_sb", [128, 4], F32)
        negc = sbuf("negc", [128, 32], F32)
        tiny = sbuf("tiny", [1, 64], F32)
        ohk = sbuf("ohk", [128, 4, 8], BF16)
        sel = sbuf("sel", [8, 32], F32)
        kmaxg = sbuf("kmaxg", [8, 9], F32)
        kqb = sbuf("kqb", [32, 1], F32)
        qn = sbuf("qn", [32, 2], F32)
        xnT_h = sbuf("xnT_h", [128, 16, HALF], BF16)
        es2 = contextlib.ExitStack()
        sbuf2 = lambda name, shape, dt: es2.enter_context(nc.sbuf_tensor(name, shape, dt))
        kT = sbuf2("kT", [128, 4, S], BF16)
        vaug = sbuf2("vaug", [128, 32, 8, 65], BF16)
        wbuf = sbuf2("wbuf", [128, 2, 16, 128], BF16)
        sqb = sbuf2("sqb", [128, 2, 512], BF16)

        sCst = Cnt(es, nc, "sCst")
        sCstG = Cnt(es, nc, "sCstG")
        sXB = [Cnt(es, nc, f"sXB{i}") for i in range(2)]
        sXB2 = [Cnt(es, nc, f"sXB2{i}") for i in range(2)]
        sXB3 = [Cnt(es, nc, f"sXB3{i}") for i in range(2)]
        sXB4 = [Cnt(es, nc, f"sXB4{i}") for i in range(2)]
        sWB = [Cnt(es, nc, f"sWB{i}") for i in range(2)]
        sEB = [Cnt(es, nc, f"sEB{i}") for i in range(2)]
        sAM = [Cnt(es, nc, f"sAM{i}") for i in range(2)]
        sGW = [Cnt(es, nc, f"sGW{i}") for i in range(2)]
        sDW = [Cnt(es, nc, f"sDW{i}") for i in range(2)]
        sSB = [Cnt(es, nc, f"sSB{i}") for i in range(2)]
        sXL = [Cnt(es, nc, f"sXL{i}") for i in range(2)]
        sO = [Cnt(es, nc, f"sO{i}") for i in range(2)]
        sXH = Cnt(es, nc, "sXH")
        sSm = [Cnt(es, nc, f"sSm{i}") for i in range(2)]
        sSp = Cnt(es, nc, "sSp")
        sT = Cnt(es, nc, "sT")
        sT2 = Cnt(es, nc, "sT2")
        sS = Cnt(es, nc, "sS")
        sS2 = Cnt(es, nc, "sS2")
        sV = Cnt(es, nc, "sV")
        sV2 = Cnt(es, nc, "sV2")
        sG = Cnt(es, nc, "sG")
        sOut = Cnt(es, nc, "sOut")
        sBar = Cnt(es, nc, "sBar")

        def barrier():
            base = sBar.v
            for eng in Plan.ENGS:
                P.op(eng, lambda e: e.nop(), inc=sBar)
            for eng in Plan.ENGS:
                P.wait(eng, sBar, base + 5)
            free.clear()

        P.op("gpsimd", lambda e: e.dma_start(out=eye[:], in_=EYE), inc=sCstG, n=16)
        P.op("sync", lambda e: e.dma_start(out=eyef[:], in_=EYE), inc=sCst, n=16)
        P.op("sync", lambda e: e.dma_start(out=sel[:], in_=SEL), inc=sCst, n=16)
        for t in range(16):
            P.op("sync", lambda e, t=t: e.dma_start(
                out=bq8[:, t:t + 1], in_=BQ[t:t + 1, :].rearrange("o p -> p o")), inc=sCst, n=16)
        for t in range(4):
            P.op("sync", lambda e, t=t: e.dma_start(
                out=bk_sb[:, t:t + 1], in_=BK[t:t + 1, :].rearrange("o p -> p o")), inc=sCst, n=16)
            P.op("sync", lambda e, t=t: e.dma_start(
                out=bv_sb[:, t:t + 1], in_=BV[t:t + 1, :].rearrange("o p -> p o")), inc=sCst, n=16)
        P.wait("vector", sCst)
        P.wait("vector", sCstG)
        P.op("vector", lambda e: e.memset(onesf[:], 1.0), inc=sV)
        P.op("vector", lambda e: e.memset(ones_b16[:], 1.0), inc=sV)
        P.op("vector", lambda e: e.memset(vaug[:, :, :, 64:65], 1.0), inc=sV)
        P.op("vector", lambda e: e.memset(ohk[:], 0.0), inc=sV)
        P.wait("vector", sV)
        for t in range(4):
            P.op("vector", lambda e, t=t: e.memset(ohk[0:64, t, 2 * t:2 * t + 1], 1.0), inc=sV)
            P.op("vector", lambda e, t=t: e.memset(ohk[64:128, t, 2 * t + 1:2 * t + 2], 1.0), inc=sV)
        P.op("vector", lambda e: e.tensor_scalar(bq8[:], bq8[:], 0.125, None, ALU.mult), inc=sV)
        bq8_ready = sV.v

        es1 = contextlib.ExitStack()
        sbuf1 = lambda name, shape, dt: es1.enter_context(nc.sbuf_tensor(name, shape, dt))
        xbuf = sbuf1("xbuf", [128, 7, D], F32)
        xnbuf = sbuf1("xnbuf", [128, 2, D], F32)
        st6 = sbuf1("st6", [128, 4, 4, 6], F32)
        stat = sbuf1("stat", [128, 4, 8], F32)
        xnT_t = sbuf1("xnT_t", [128, 16, 512], BF16)
        vtmp = sbuf1("vtmp", [128, 2, 512], F32)

        pp = [es.enter_context(nc.psum_tensor(f"pp{i}", [128, 2, 512], F32)) for i in range(3)]
        pp3 = es.enter_context(nc.psum_tensor("pp3", [128, 512], F32))
        pp4 = es.enter_context(nc.psum_tensor("pp4", [128, 512], F32))

        def bank(i):
            if i < 6:
                return pp[i // 2][:, i % 2, :]
            return pp3[:] if i == 6 else pp4[:]

        xq_v = {}
        S7 = sXB + sXB2 + sXB3 + sXB4

        def quad_dma(grp):
            rows0 = grp * 512
            par = grp % 2
            for ti in range(4):
                buf = (4 * grp + ti) % 7
                q = "sync" if ti < 2 else "gpsimd"
                sem = ([sXB, sXB3] if ti < 2 else [sXB2, sXB4])[par][ti % 2]
                wait_free(q, ("xbuf", buf))
                v = P.op(q, lambda e, r0=rows0 + ti * 128, buf=buf: e.dma_start(
                    out=xbuf[:, buf, :], in_=X[r0:r0 + 128, :]), inc=sem, n=16)
                xq_v[(grp, ti)] = (sem, v)

        def ln_quad(grp, dst, dst_off):
            wait_free("vector", ("stat",))
            for ti in range(4):
                buf = (4 * grp + ti) % 7
                P.wait("vector", *xq_v.pop((grp, ti)))
                for i in range(4):
                    P.op("vector", lambda e, ti=ti, i=i, buf=buf: e.bn_stats(
                        st6[:, ti, i, :], xbuf[:, buf, i * 512:(i + 1) * 512]),
                        inc=(sV if i == 3 else None))
            P.wait("vector", sV)
            for ti in range(4):
                P.op("vector", lambda e, ti=ti: e.bn_aggr(
                    stat[:, ti, 0:2], st6[:, ti].rearrange("p a b -> p (a b)")),
                    inc=(sV if ti == 3 else None))
            P.wait("vector", sV)
            sv = lambda c: stat[:, :, c:c + 1].rearrange("p a o -> p (a o)")
            P.op("vector", lambda e: e.tensor_scalar(sv(1), sv(1), EPS, None, ALU.add), inc=sV)
            P.wait("vector", sV)
            P.wait("scalar", sV)
            P.op("scalar", lambda e: e.activation(sv(2), sv(1), AF.Sqrt), inc=sS)
            P.wait("vector", sS)
            P.op("vector", lambda e: e.reciprocal(sv(3), sv(2)), inc=sV)
            P.wait("vector", sV)
            P.op("vector", lambda e: e.scalar_tensor_tensor(
                sv(4), sv(0), -1.0, sv(3), ALU.mult, ALU.mult), inc=sV)
            P.wait("vector", sV)
            v_xn = 0
            for ti in range(4):
                wait_free("vector", ("xnbuf", ti % 2))
                v_xn = P.op("vector", lambda e, ti=ti, buf=(4 * grp + ti) % 7: e.tensor_scalar(
                    xnbuf[:, ti % 2, :], xbuf[:, buf, :], stat[:, ti, 3:4], stat[:, ti, 4:5],
                    ALU.mult, ALU.add), inc=sV)
                free_set(("xbuf", (4 * grp + ti) % 7), (sV, v_xn))
                P.wait("tensor", sV, v_xn)
                for k4 in range(4):
                    tb = 4 + (k4 % 2)
                    wait_free("tensor", ("bank", tb))
                    for kk in range(4):
                        k = k4 * 4 + kk
                        P.op("tensor", lambda e, k=k, kk=kk, tb=tb, ti=ti: e.transpose(
                            bank(tb)[:, kk * 128:(kk + 1) * 128],
                            xnbuf[:, ti % 2, k * 128:(k + 1) * 128], eyef[:]),
                            inc=(sT if kk == 3 else None))
                    P.wait("scalar", sT)
                    do = dst_off + ti * 128
                    v_e = P.op("scalar", lambda e, k4=k4, tb=tb, dst=dst, do=do: e.activation(
                        dst[:, k4 * 4:k4 * 4 + 4, do:do + 128],
                        bank(tb)[:].rearrange("p (a b) -> p a b", a=4), AF.Identity), inc=sV2)
                    free_set(("bank", tb), (sV2, v_e))
                free_set(("xnbuf", ti % 2), (sV2, sV2.v))
            free_set(("stat",), (sV, v_xn))

        def kv_group(src, src_off, tok0):
            for which in ("k", "v"):
                wsrc = WK if which == "k" else WV
                bsrc = bk_sb if which == "k" else bv_sb
                for m in range(4):
                    psm = bank(m)
                    wait_free("gpsimd", ("wbuf", m % 2))
                    P.op("gpsimd", lambda e, m=m, wsrc=wsrc: e.dma_start(
                        out=wbuf[:, m % 2, :, :], in_=wsrc[m]), inc=sWB[m % 2], n=16)
                    P.wait("tensor", sWB[m % 2])
                    wait_free("tensor", ("bank", m))
                    for k in range(16):
                        P.op("tensor", lambda e, m=m, k=k, psm=psm, src=src, src_off=src_off: e.matmul(
                            psm, wbuf[:, m % 2, k, :], src[:, k, src_off:src_off + 512],
                            start=(k == 0), stop=(k == 15)), inc=(sT if k == 15 else None))
                    v_mm = sT.v
                    free_set(("wbuf", m % 2), (sT, v_mm))
                    P.wait("scalar", sT, v_mm)
                    if which == "k":
                        v_ev = P.op("scalar", lambda e, m=m, psm=psm, tok0=tok0, bsrc=bsrc: e.activation(
                            kT[:, m, tok0:tok0 + 512], psm, AF.Identity, bias=bsrc[:, m:m + 1]), inc=sS)
                        free_set(("bank", m), (sS, v_ev))
                    else:
                        wait_free("scalar", ("vtmp", m % 2))
                        v_ev = P.op("scalar", lambda e, m=m, psm=psm, bsrc=bsrc: e.activation(
                            vtmp[:, m % 2, :], psm, AF.Identity, bias=bsrc[:, m:m + 1]), inc=sS)
                        free_set(("bank", m), (sS, v_ev))
                        P.wait("tensor", sS, v_ev)
                        bb = 4 + (m % 2)
                        wait_free("tensor", ("bank", bb))
                        for tb in range(4):
                            P.op("tensor", lambda e, m=m, tb=tb, bb=bb: e.transpose(
                                bank(bb)[:, tb * 128:(tb + 1) * 128],
                                vtmp[:, m % 2, tb * 128:(tb + 1) * 128], eyef[:]),
                                inc=(sT if tb == 3 else None))
                        P.wait("scalar", sT)
                        kb0 = tok0 // 128
                        v_e = P.op("scalar", lambda e, kb0=kb0, m=m, bb=bb: e.activation(
                            vaug[:, kb0:kb0 + 4, 2 * m:2 * m + 2, 0:64],
                            bank(bb)[:].rearrange("p (a c b) -> p a c b", a=4, c=2), AF.Identity), inc=sV2)
                        free_set(("bank", bb), (sV2, v_e))
                        free_set(("vtmp", m % 2), (sV2, sV2.v))

        def q_proj(qcol, ready_cnt=None, ready_v=None):
            P.wait("vector", sV, bq8_ready)
            red = []
            for t in range(16):
                wait_free("gpsimd", ("wbuf", t % 2))
                P.op("gpsimd", lambda e, t=t: e.dma_start(out=wbuf[:, t % 2, :, :], in_=WQ[t]), inc=sWB[t % 2], n=16)
                P.wait("tensor", sWB[t % 2])
                if ready_cnt is not None:
                    P.wait("tensor", ready_cnt, ready_v)
                bslot = t % 2
                pq = bank(4 + bslot)
                wait_free("tensor", ("bank", 4 + bslot))
                wait_free("tensor", ("pv", bslot))
                for k in range(16):
                    P.op("tensor", lambda e, t=t, k=k, pq=pq, qcol=qcol: e.matmul(
                        pq, wbuf[:, t % 2, k, :], xnT_h[:, k, qcol:qcol + 512],
                        start=(k == 0), stop=(k == 15)), inc=(sT if k == 15 else None))
                v_mm = sT.v
                free_set(("wbuf", t % 2), (sT, v_mm))
                P.wait("scalar", sT, v_mm)
                v_ev = P.op("scalar", lambda e, t=t, pq=pq: e.activation(
                    qT[:, t, :], pq, AF.Identity,
                    bias=bq8[:, t:t + 1], scale=0.125), inc=sS)
                free_set(("bank", 4 + bslot), (sS, v_ev))
                free_set(("pv", bslot), (sS, v_ev))
                P.wait("scalar", sS, v_ev)
                sl = t % 2
                wait_free("scalar", ("sqb", sl))
                v_sq = P.op("scalar", lambda e, t=t, sl=sl: e.activation(
                    sqb[:, sl, 0:512], qT[:, t, :], AF.Square), inc=sS)
                P.wait("tensor", sS, v_sq)
                if t == 0:
                    P.wait("tensor", sV, ohq_ready)
                    wait_free("tensor", ("bank", 7))
                P.op("tensor", lambda e, t=t, sl=sl: e.matmul(
                    pp4[0:32, :], ohq[:, t, :], sqb[:, sl, 0:512],
                    start=(t == 0), stop=(t == 15)), inc=sT)
                free_set(("sqb", sl), (sT, sT.v))
            P.wait("vector", sT, sT.v)
            v_qr = P.op("vector", lambda e: e.tensor_reduce(
                qn[0:32, 0:1], pp4[0:32, :], axis=AX, op=ALU.max), inc=sV)
            free_set(("bank", 7), (sV, v_qr))
            return sS.v

        def compute_negc(qc_idx):
            P.wait("vector", sV)
            v_p = P.op("vector", lambda e: e.tensor_tensor(
                qn[0:32, 1:2], qn[0:32, 0:1], kqb[0:32, 0:1], ALU.mult), inc=sV)
            P.wait("scalar", sV, v_p)
            v_sq = P.op("scalar", lambda e: e.activation(
                qn[0:32, 1:2], qn[0:32, 1:2], AF.Sqrt), inc=sS)
            P.wait("vector", sS, v_sq)
            v_n = P.op("vector", lambda e: e.tensor_scalar(
                qn[0:32, 1:2], qn[0:32, 1:2], -1.0, None, ALU.mult), inc=sV)
            P.wait("tensor", sV, v_n)
            wait_free("tensor", ("bank", 7))
            v_tp = P.op("tensor", lambda e: e.transpose(
                pp4[0:1, 0:32], qn[0:32, 1:2], eyef[0:32, 0:32]), inc=sT)
            P.wait("vector", sT, v_tp)
            v_cp = P.op("vector", lambda e: e.tensor_copy(tiny[0:1, 0:32], pp4[0:1, 0:32]), inc=sV)
            P.wait("tensor", sV, v_cp)
            v_bc = P.op("tensor", lambda e: e.matmul(
                pp4[:, 0:32], onesf[0:1, 0:128], tiny[0:1, 0:32],
                start=True, stop=True), inc=sT)
            P.wait("vector", sT, v_bc)
            P.wait("vector", sS2)
            v_nc = P.op("vector", lambda e: e.tensor_copy(negc[:, 0:32], pp4[:, 0:32]), inc=sV)
            free_set(("bank", 7), (sV, v_nc))
            negc_ev[0] = v_nc
            if dbg:
                P.wait("sync", sV, v_nc)
                P.op("sync", lambda e, hi=qc_idx: e.dma_start(out=DNC[hi], in_=negc[:]), inc=sOut, n=16)

        negc_ev = [0]
        P.wait("tensor", sCstG)
        P.wait("tensor", sCst)
        tile_i = [0]
        xnT_ready = {}

        def knorm_group(grp):
            nb = 6 if grp % 2 == 0 else 7
            P.wait("tensor", sV, bq8_ready)
            wait_free("tensor", ("bank", nb))
            if nb == 6:
                wait_free("tensor", ("bc",))
            for t in range(4):
                sl = (grp * 4 + t) % 2
                wait_free("scalar", ("sqb", sl))
                v_sq = P.op("scalar", lambda e, t=t, grp=grp, sl=sl: e.activation(
                    sqb[:, sl, 0:512], kT[:, t, grp * 512:grp * 512 + 512], AF.Square), inc=sS)
                P.wait("tensor", sS, v_sq)
                P.op("tensor", lambda e, t=t, sl=sl, nb=nb: e.matmul(
                    bank(nb)[0:8, :], ohk[:, t, :], sqb[:, sl, 0:512],
                    start=(t == 0), stop=(t == 3)), inc=sT)
                free_set(("sqb", sl), (sT, sT.v))
            P.wait("vector", sT, sT.v)
            v_r = P.op("vector", lambda e, nb=nb, grp=grp: e.tensor_reduce(
                kmaxg[0:8, grp:grp + 1], bank(nb)[0:8, :], axis=AX, op=ALU.max), inc=sV)
            free_set(("bank", nb), (sV, v_r))
            if nb == 6:
                free_set(("bc",), (sV, v_r))

        def do_group(grp):
            tok0 = grp * 512
            if grp < 2:
                dst, off = xnT_h, grp * 512
            else:
                dst, off = xnT_t, 0
                wait_free("vector", ("xnT_t",))
            if grp == 0:
                quad_dma(0)
            ln_quad(grp, dst, off)
            ready = sV2.v
            P.wait("tensor", sV2, ready)
            kv_group(dst, off, tok0)
            if grp < 7:
                quad_dma(grp + 1)
            ev = [(sT, sT.v)]
            if grp < 4:
                hh, c0 = grp // 2, (grp % 2) * 512
                P.wait("sync", sV2, ready)
                v_sp = P.op("sync", lambda e, hh=hh, c0=c0, dst=dst, off=off: e.dma_start(
                    out=SPXNT[hh, :, :, c0:c0 + 512], in_=dst[:, :, off:off + 512]), inc=sSp, n=16)
                ev.append((sSp, v_sp))
            if grp >= 2:
                free_set(("xnT_t",), *ev)
            xnT_ready[grp] = ready
            if grp >= 1:
                knorm_group(grp - 1)

        for grp in range(8):
            do_group(grp)
        knorm_group(7)

        P.wait("vector", sV)
        P.op("vector", lambda e: e.tensor_reduce(
            kmaxg[0:8, 8:9], kmaxg[0:8, 0:8], axis=AX, op=ALU.max), inc=sV)
        P.wait("tensor", sV)
        wait_free("tensor", ("bank", 7))
        v_selmm = P.op("tensor", lambda e: e.matmul(
            pp4[0:32, 0:1], sel[0:8, :], kmaxg[0:8, 8:9], start=True, stop=True), inc=sT)
        P.wait("vector", sT, v_selmm)
        v_kqb = P.op("vector", lambda e: e.tensor_copy(kqb[:, 0:1], pp4[0:32, 0:1]), inc=sV)
        free_set(("bank", 7), (sV, v_kqb))

        if dbg:
            P.wait("sync", sS)
            P.wait("sync", sV2)
            P.op("sync", lambda e: e.dma_start(out=DKT, in_=kT[:]), inc=sOut, n=16)
            P.op("sync", lambda e: e.dma_start(out=DVA, in_=vaug[:]), inc=sOut, n=16)

        barrier()
        es1.close()

        es3 = contextlib.ExitStack()
        sbuf3 = lambda name, shape, dt: es3.enter_context(nc.sbuf_tensor(name, shape, dt))
        qT = sbuf3("qT", [128, 16, 512], BF16)
        ohq = sbuf3("ohq", [128, 16, 32], BF16)
        ebl = sbuf3("ebl", [128, 2, NEBL, 512], BF16)
        etmp = sbuf3("etmp", [128, 4, 1024], BF16)
        ebig = sbuf3("ebig", [128, 3, 8, 512], BF16)
        rz = sbuf3("rz", [1, 2, 512], F32)
        rzbf = sbuf3("rzbf", [1, 2, 512], BF16)
        rzbs = sbuf3("rzbs", [64, 2, 512], BF16)
        aoT = sbuf3("aoT", [128, 16, 512], BF16)

        def head_of_qcol(j):
            return 4 * (2 * (j // 8) + (j % 2)) + (j % 8) // 2

        P.op("vector", lambda e: e.memset(ohq[:], 0.0), inc=sV)
        P.wait("vector", sV)
        for t in range(16):
            h0, h1 = head_of_qcol(2 * t), head_of_qcol(2 * t + 1)
            P.op("vector", lambda e, t=t, h0=h0: e.memset(ohq[0:64, t, h0:h0 + 1], 1.0), inc=sV)
            P.op("vector", lambda e, t=t, h1=h1: e.memset(ohq[64:128, t, h1:h1 + 1], 1.0), inc=sV)
        ohq_ready = sV.v

        pvp = [pp[2][0:65, i, :] for i in range(2)]
        pvz = [pp[2][64:65, i, :] for i in range(2)]
        pvo = [pp[2][0:64, i, :] for i in range(2)]
        bcp = pp3[0:64, :]

        def pair_entries(qcbase):
            ents = sorted(kb_entries(qcbase), key=lambda x: x[1])
            out, i = [], 0
            while i < len(ents):
                if i + 1 < len(ents) and ents[i + 1][1] == ents[i][1] + 1:
                    out.append((ents[i], ents[i + 1]))
                    i += 2
                else:
                    out.append((ents[i], None))
                    i += 1
            return out

        attn_hi = [0]

        def attn_all():
            items = [(qc, (qc % 2) * 512, g) for qc in range(4) for g in range(NG)]
            ld = {}

            def load_item(j):
                slot = j % 2
                g = items[j][2]
                wait_free("gpsimd", ("ebl", slot))
                P.op("gpsimd", lambda e, g=g, slot=slot: e.dma_start(
                    out=ebl[:, slot, :, :], in_=EBL[g]), inc=sEB[slot], n=16)
                ld[j] = (sEB[slot], sEB[slot].v)

            load_item(0)
            P.wait("vector", sSp)
            for j, (qi, qcb, g) in enumerate(items):
                if j + 1 < len(items):
                    load_item(j + 1)
                if g == 0:
                    if dbg:
                        P.wait("scalar", sOut)
                    if qi == 2:
                        P.wait("sync", sT)
                        P.wait("sync", sSp)
                        P.op("sync", lambda e: e.dma_start(out=xnT_h[:], in_=SPXNT[1]), inc=sXH, n=16)
                    if qi >= 2:
                        q_proj((qi % 2) * 512, sXH, sXH.v)
                    else:
                        q_proj((qi % 2) * 512)
                    if dbg:
                        P.wait("sync", sS)
                        P.op("sync", lambda e, qi=qi: e.dma_start(out=DQT[qi], in_=qT[:]), inc=sOut, n=16)
                    compute_negc(qi)
                    wait_free("vector", ("aoT",))
                slot_e = j % 2
                prs = pair_entries(qi * 512)
                npr = len(prs)
                P.wait("tensor", ld[j][0], ld[j][1])
                P.wait("gpsimd", ld[j][0], ld[j][1])
                P.wait("vector", ld[j][0], ld[j][1])

                pair_ctr = [0]
                scinfo = {}

                def scpart(kv):
                    h = kv * NG + g
                    tq, base = q_tile_of_head(h)
                    ktile, kbase = kv // 2, 64 * (kv % 2)
                    info = []
                    for pi, ((kb_a, tix_a), eb) in enumerate(prs):
                        ss = pair_ctr[0] % 2
                        es4 = pair_ctr[0] % 4
                        pair_ctr[0] += 1
                        width = 1024 if eb else 512
                        wait_free("tensor", ("scp", ss))
                        if eb:
                            P.op("tensor", lambda e, kb=kb_a, ss=ss, ktile=ktile, kbase=kbase,
                                 tq=tq, base=base: e.matmul(
                                pp[ss][:, 0, :], kT[kbase:kbase + 64, ktile, kb * 128:(kb + 1) * 128],
                                qT[base:base + 64, tq, :], start=True, stop=True))
                            v_sc = P.op("tensor", lambda e, kb=eb[0], ss=ss, ktile=ktile, kbase=kbase,
                                        tq=tq, base=base: e.matmul(
                                pp[ss][:, 1, :], kT[kbase:kbase + 64, ktile, kb * 128:(kb + 1) * 128],
                                qT[base:base + 64, tq, :], start=True, stop=True), inc=sT)
                        else:
                            v_sc = P.op("tensor", lambda e, kb=kb_a, ss=ss, ktile=ktile, kbase=kbase,
                                        tq=tq, base=base: e.matmul(
                                pp[ss][:, 0, :], kT[kbase:kbase + 64, ktile, kb * 128:(kb + 1) * 128],
                                qT[base:base + 64, tq, :], start=True, stop=True), inc=sT)
                        P.wait("scalar", sT, v_sc)
                        P.wait("scalar", sV, negc_ev[0])
                        wait_free("scalar", ("etmp", es4))
                        v_ex = P.op("scalar", lambda e, es4=es4, ss=ss, w=width, h=h: e.activation(
                            etmp[:, es4, 0:w],
                            pp[ss][:].rearrange("p a b -> p (a b)")[:, 0:w],
                            AF.Exp, bias=negc[:, h:h + 1]), inc=sS2)
                        free_set(("scp", ss), (sS2, v_ex))
                        info.append((pi, es4, width, tix_a, eb, v_ex))
                    scinfo[kv] = info

                def multpart(kv):
                    eslot = kv % 3
                    mu_ev = []
                    for (pi, es4, width, tix_a, eb, v_ex) in scinfo.pop(kv):
                        meng, mcnt = ("gpsimd", sG) if (pi % 2 == 0) else ("vector", sV2)
                        P.wait(meng, sS2, v_ex)
                        wait_free(meng, ("ebig", eslot))
                        v_mu = P.op(meng, lambda e, es4=es4, w=width, tix=tix_a, slot_e=slot_e,
                                    eslot=eslot, pi=pi, nt=(2 if eb else 1): e.tensor_tensor(
                            ebig[:, eslot, 2 * pi:2 * pi + nt, :].rearrange("p a b -> p (a b)"),
                            etmp[:, es4, 0:w],
                            ebl[:, slot_e, tix:tix + nt, :].rearrange("p a b -> p (a b)"),
                            ALU.mult), inc=mcnt)
                        free_set(("etmp", es4), (mcnt, v_mu))
                        mu_ev.append((mcnt, v_mu))
                    return mu_ev

                def pvpart(kv, mu_ev):
                    h = kv * NG + g
                    eslot = kv % 3
                    pvs = attn_hi[0] % 2
                    for (cnt, v) in mu_ev:
                        P.wait("tensor", cnt, v)
                    wait_free("tensor", ("pv", pvs))
                    wait_free("tensor", ("bank", 4 + pvs))
                    kbs = []
                    for (kb_a, tix_a), eb in prs:
                        kbs.append(kb_a)
                        if eb:
                            kbs.append(eb[0])
                    nk = len(kbs)
                    for ki, kb in enumerate(kbs):
                        P.op("tensor", lambda e, kb=kb, kv=kv, ki=ki, eslot=eslot, pvs=pvs,
                             first=(ki == 0), last=(ki == nk - 1): e.matmul(
                            pvp[pvs], vaug[:, kb, kv, :], ebig[:, eslot, ki, :],
                            start=first, stop=last), inc=(sT2 if ki == nk - 1 else None))
                    free_set(("ebig", eslot), (sT2, sT2.v))
                    v_pvlast = sT2.v
                    P.wait("scalar", sT2, v_pvlast)
                    wait_free("scalar", ("rz", pvs))
                    P.op("scalar", lambda e, pvs=pvs: e.activation(
                        rz[0:1, pvs, :], pvz[pvs], AF.Ln), inc=sS2)
                    P.wait("scalar", sS2)
                    wait_free("scalar", ("rzbf", pvs))
                    v_rzf = P.op("scalar", lambda e, pvs=pvs: e.activation(
                        rzbf[0:1, pvs, :], rz[0:1, pvs, :], AF.Exp, scale=-1.0), inc=sS2)
                    free_set(("rz", pvs), (sS2, v_rzf))
                    pend.append((h, pvs, v_rzf))
                    attn_hi[0] += 1

                pend = []

                def flush_epi():
                    (h_p, pvs_p, v_rzf_p) = pend.pop(0)
                    P.wait("tensor", sS2, v_rzf_p)
                    wait_free("tensor", ("bc",))
                    wait_free("tensor", ("bank", 6))
                    v_bc = P.op("tensor", lambda e, pvs=pvs_p: e.matmul(
                        bcp, ones_b16[0:1, 0:64], rzbf[0:1, pvs, :], start=True, stop=True), inc=sT)
                    free_set(("rzbf", pvs_p), (sT, v_bc))
                    P.wait("vector", sT, v_bc)
                    wait_free("vector", ("rzbs", pvs_p))
                    v_cp = P.op("vector", lambda e, pvs=pvs_p: e.tensor_copy(rzbs[:, pvs, :], bcp), inc=sV)
                    free_set(("bc",), (sV, v_cp))
                    free_set(("bank", 6), (sV, v_cp))
                    P.wait("vector", sV, v_cp)
                    ta, abase = h_p // 2, 64 * (h_p % 2)
                    v_ao = P.op("vector", lambda e, ta=ta, abase=abase, pvs=pvs_p: e.tensor_tensor(
                        aoT[abase:abase + 64, ta, :], pvo[pvs], rzbs[:, pvs, :], ALU.mult), inc=sV)
                    free_set(("pv", pvs_p), (sV, v_ao))
                    free_set(("bank", 4 + pvs_p), (sV, v_ao))
                    free_set(("rzbs", pvs_p), (sV, v_ao))

                scpart(0); mu0 = multpart(0)
                scpart(1); mu1 = multpart(1)
                mu = {0: mu0, 1: mu1}
                for kv in (0, 2, 4, 6):
                    if kv + 2 < 8:
                        scpart(kv + 2)
                        mu[kv + 2] = multpart(kv + 2)
                    if kv + 3 < 8:
                        scpart(kv + 3)
                    pvpart(kv, mu.pop(kv))
                    if kv + 3 < 8:
                        mu[kv + 3] = multpart(kv + 3)
                    pvpart(kv + 1, mu.pop(kv + 1))
                    while pend:
                        flush_epi()
                while pend:
                    flush_epi()
                free_set(("ebl", slot_e), (sG, sG.v), (sV2, sV2.v))
                if g == NG - 1:
                    P.wait("sync", sV)
                    v_sp = P.op("sync", lambda e, qi=qi: e.dma_start(out=SPAOT[qi], in_=aoT[:]), inc=sSp, n=16)
                    free_set(("aoT",), (sSp, v_sp))

        attn_all()
        barrier()

        es3.close()
        es2.close()
        es4 = contextlib.ExitStack()
        sbuf4 = lambda name, shape, dt: es4.enter_context(nc.sbuf_tensor(name, shape, dt))
        woT = sbuf4("woT", [128, 16, D], BF16)
        aom = sbuf4("aom", [128, 2, 16, 512], BF16)
        sm = sbuf4("sm", [128, 2, D], BF16)

        P.wait("gpsimd", sSp)
        if dbg:
            P.wait("gpsimd", sOut)
            P.wait("sync", sOut)
        for k in range(16):
            P.op("gpsimd", lambda e, k=k: e.dma_start(out=woT[:, k, :], in_=WO[k]), inc=sCstG, n=16)
        P.wait("tensor", sCstG)
        P.wait("sync", sSp)
        P.op("sync", lambda e: e.dma_start(out=aom[:, 0, :, :], in_=SPAOT[0]), inc=sAM[0], n=16)
        for qi in range(4):
            if qi + 1 < 4:
                wait_free("sync", ("aom", (qi + 1) % 2))
                P.op("sync", lambda e, qi=qi: e.dma_start(
                    out=aom[:, (qi + 1) % 2, :, :], in_=SPAOT[qi + 1]), inc=sAM[(qi + 1) % 2], n=16)
            P.wait("tensor", sAM[qi % 2])
            for mm_ in range(4):
                m = qi * 4 + mm_
                for n in range(4):
                    pso = bank(n)
                    wait_free("tensor", ("bank", n))
                    for k in range(16):
                        P.op("tensor", lambda e, qi=qi, mm_=mm_, n=n, k=k, pso=pso: e.matmul(
                            pso, aom[:, qi % 2, k, mm_ * 128:(mm_ + 1) * 128],
                            woT[:, k, n * 512:(n + 1) * 512],
                            start=(k == 0), stop=(k == 15)), inc=(sT if k == 15 else None))
                    v_mm = sT.v
                    P.wait("scalar", sT, v_mm)
                    if n == 0:
                        wait_free("scalar", ("sm", m % 2))
                    v_ev = P.op("scalar", lambda e, m=m, n=n, pso=pso: e.activation(
                        sm[:, m % 2, n * 512:(n + 1) * 512], pso, AF.Identity), inc=sS)
                    free_set(("bank", n), (sS, v_ev))
                if mm_ == 3:
                    free_set(("aom", qi % 2), (sT, sT.v))
                P.wait("sync", sS)
                v_sp = P.op("sync", lambda e, m=m: e.dma_start(out=SPS[m], in_=sm[:, m % 2, :]), inc=sSm[m % 2], n=16)
                free_set(("sm", m % 2), (sSm[m % 2], v_sp))

        barrier()
        bank8 = bank

        def pair(i):
            return pp[i][:].rearrange("p a b -> p (a b)")

        es4.close()
        P.wait("gpsimd", sSm[0])
        P.wait("gpsimd", sSm[1])
        P.wait("sync", sSm[0])
        P.wait("sync", sSm[1])
        rtr = sbuf("rtr", [128, 16, 8], BF16)
        lgf = sbuf("lgf", [8, 1024], F32)
        logT = sbuf("logT", [128, 8, 8], F32)
        idx8 = sbuf("idx8", [128, 8], F32)
        idxB = sbuf("idxB", [128, 8], F32)
        msk = sbuf("msk", [128, 8], F32)
        m1k = sbuf("m1k", [128, 8], F32)
        m2k = sbuf("m2k", [128, 8], F32)
        tmp8 = sbuf("tmp8", [128, 8], F32)
        r6 = sbuf("r6", [128, 6], F32)
        combT = sbuf("combT", [128, 8, 8], F32)
        gupw = sbuf("gupw", [128, 2, 16, 256], BF16)
        dwnw = sbuf("dwnw", [128, 2, 11, 512], BF16)
        he = sbuf("he", [128, 11, HALF], BF16)
        silu_t = sbuf("silu_t", [128, 2, 1024], BF16)
        acc = sbuf("acc", [128, 8, D], F32)
        gbuf = sbuf("gbuf", [128, D], F32)
        sob = sbuf("sob", [128, 2, D], BF16)
        sobf = sbuf("sobf", [128, D], F32)
        outb = sbuf("outb", [128, 2, D], F32)

        P.op("gpsimd", lambda e: e.dma_start(out=rtr[:], in_=RTR), inc=sCstG, n=16)
        v_rtr = sCstG.v
        v_iota = P.op("gpsimd", lambda e: e.iota(idx8[:], pattern=[[1, 8]], base=0, channel_multiplier=0,
                                                 allow_small_or_imprecise_dtypes=True), inc=sG)
        BIGC = 65536.0
        P.wait("vector", sG, v_iota)
        P.op("vector", lambda e: e.tensor_scalar(idxB[:], idx8[:], BIGC, None, ALU.add), inc=sV)

        def vop(fn):
            P.op("vector", fn, inc=sV)
            P.wait("vector", sV)

        def router(hh):
            if dbg:
                P.wait("vector", sOut)
            P.wait("tensor", sCstG, v_rtr)
            for n in range(2):
                wait_free("tensor", ("dbank", 4))
                for k in range(16):
                    P.op("tensor", lambda e, n=n, k=k: e.matmul(
                        bank8(4)[0:8, :], rtr[:, k, :], xnT_h[:, k, n * 512:(n + 1) * 512],
                        start=(k == 0), stop=(k == 15)), inc=(sT if k == 15 else None))
                P.wait("vector", sT)
                v_ev = P.op("vector", lambda e, n=n: e.tensor_copy(
                    lgf[0:8, n * 512:(n + 1) * 512], bank8(4)[0:8, :]), inc=sV)
                free_set(("dbank", 4), (sV, v_ev))
            P.wait("tensor", sV)
            for tb in range(8):
                wait_free("tensor", ("dbank", 5))
                v_t = P.op("tensor", lambda e, tb=tb: e.transpose(
                    bank8(5)[0:128, 0:8], lgf[0:8, tb * 128:(tb + 1) * 128], eyef[0:8, 0:8]), inc=sT)
                P.wait("vector", sT, v_t)
                v_e = P.op("vector", lambda e, tb=tb: e.tensor_copy(logT[:, tb, :], bank8(5)[0:128, 0:8]), inc=sV)
                free_set(("dbank", 5), (sV, v_e))
            for tb in range(8):
                L = logT[:, tb, :]
                vop(lambda e, L=L: e.tensor_reduce(r6[:, 0:1], L, axis=AX, op=ALU.max))
                vop(lambda e, L=L: e.tensor_scalar(msk[:], L, r6[:, 0:1], None, ALU.is_equal))
                vop(lambda e: e.scalar_tensor_tensor(tmp8[:], msk[:], -BIGC, idxB[:], ALU.mult, ALU.add))
                vop(lambda e: e.tensor_reduce(r6[:, 1:2], tmp8[:], axis=AX, op=ALU.min))
                vop(lambda e: e.tensor_scalar(m1k[:], idx8[:], r6[:, 1:2], None, ALU.is_equal))
                vop(lambda e, L=L: e.scalar_tensor_tensor(tmp8[:], m1k[:], -BIGC, L, ALU.mult, ALU.add))
                vop(lambda e: e.tensor_reduce(r6[:, 2:3], tmp8[:], axis=AX, op=ALU.max))
                vop(lambda e: e.tensor_scalar(msk[:], tmp8[:], r6[:, 2:3], None, ALU.is_equal))
                vop(lambda e: e.scalar_tensor_tensor(tmp8[:], msk[:], -BIGC, idxB[:], ALU.mult, ALU.add))
                vop(lambda e: e.tensor_reduce(r6[:, 3:4], tmp8[:], axis=AX, op=ALU.min))
                vop(lambda e: e.tensor_scalar(m2k[:], idx8[:], r6[:, 3:4], None, ALU.is_equal))
                vop(lambda e: e.tensor_tensor(r6[:, 4:5], r6[:, 0:1], r6[:, 2:3], ALU.subtract))
                P.wait("scalar", sV)
                P.op("scalar", lambda e: e.activation(r6[:, 4:5], r6[:, 4:5], AF.Sigmoid), inc=sS)
                P.wait("vector", sS)
                vop(lambda e: e.tensor_scalar(r6[:, 5:6], r6[:, 4:5], -1.0, 1.0, ALU.mult, ALU.add))
                vop(lambda e: e.tensor_scalar(m1k[:], m1k[:], r6[:, 4:5], None, ALU.mult))
                vop(lambda e: e.tensor_scalar(m2k[:], m2k[:], r6[:, 5:6], None, ALU.mult))
                vop(lambda e, tb=tb: e.tensor_tensor(combT[:, tb, :], m1k[:], m2k[:], ALU.add))
            if dbg:
                P.wait("sync", sV)
                P.op("sync", lambda e, hh=hh: e.dma_start(out=DCB[hh], in_=combT[:]), inc=sOut, n=16)

        def moe_experts(hh, interleave=None):
            for e_ in range(NE):
                for hg in range(11):
                    if interleave:
                        interleave.pop(0)()
                    sl = hg % 2
                    wait_free("gpsimd", ("gupw", sl))
                    P.op("gpsimd", lambda e, e_=e_, hg=hg, sl=sl: e.dma_start(
                        out=gupw[:, sl, :, :], in_=GUP[e_, hg]), inc=sGW[sl], n=16)
                    P.wait("tensor", sGW[sl])
                    gb = 0 if hg % 2 == 0 else 1
                    ub0, ub1 = (4, 5) if hg % 2 == 0 else (6, 7)
                    for db in (2 * gb, 2 * gb + 1, ub0, ub1):
                        wait_free("tensor", ("dbank", db))
                    for k in range(16):
                        P.op("tensor", lambda e, sl=sl, k=k, gb=gb: e.matmul(
                            pp[gb][:, 0, :], gupw[:, sl, k, 0:128], xnT_h[:, k, 0:512],
                            start=(k == 0), stop=(k == 15)))
                        P.op("tensor", lambda e, sl=sl, k=k, gb=gb: e.matmul(
                            pp[gb][:, 1, :], gupw[:, sl, k, 0:128], xnT_h[:, k, 512:1024],
                            start=(k == 0), stop=(k == 15)))
                        P.op("tensor", lambda e, sl=sl, k=k, ub0=ub0: e.matmul(
                            bank8(ub0), gupw[:, sl, k, 128:256], xnT_h[:, k, 0:512],
                            start=(k == 0), stop=(k == 15)))
                        P.op("tensor", lambda e, sl=sl, k=k, ub1=ub1: e.matmul(
                            bank8(ub1), gupw[:, sl, k, 128:256], xnT_h[:, k, 512:1024],
                            start=(k == 0), stop=(k == 15)), inc=(sT if k == 15 else None))
                    v_mm = sT.v
                    free_set(("gupw", sl), (sT, v_mm))
                    P.wait("scalar", sT, v_mm)
                    wait_free("scalar", ("silu", sl))
                    v_si = P.op("scalar", lambda e, sl=sl, gb=gb: e.activation(
                        silu_t[:, sl, :], pair(gb), AF.Silu), inc=sS)
                    free_set(("dbank", 2 * gb), (sS, v_si))
                    free_set(("dbank", 2 * gb + 1), (sS, v_si))
                    P.wait("vector", sS, v_si)
                    wait_free("vector", ("he", hg))
                    if hg % 2 == 0:
                        v_he = P.op("vector", lambda e, sl=sl, hg=hg: e.tensor_tensor(
                            he[:, hg, :], silu_t[:, sl, :], pair(2), ALU.mult), inc=sV2)
                        free_set(("dbank", 4), (sV2, v_he))
                        free_set(("dbank", 5), (sV2, v_he))
                    else:
                        P.op("vector", lambda e, sl=sl, hg=hg: e.tensor_tensor(
                            he[:, hg, 0:512], silu_t[:, sl, 0:512], bank8(6), ALU.mult), inc=sV2)
                        v_he = P.op("vector", lambda e, sl=sl, hg=hg: e.tensor_tensor(
                            he[:, hg, 512:1024], silu_t[:, sl, 512:1024], bank8(7), ALU.mult), inc=sV2)
                        free_set(("dbank", 6), (sV2, v_he))
                        free_set(("dbank", 7), (sV2, v_he))
                    free_set(("silu", sl), (sV2, v_he))
                v_he_all = sV2.v
                if e_ == NE - 1:
                    free_set(("xnT_h",), (sT, sT.v))
                P.wait("tensor", sV2, v_he_all)
                for n in range(4):
                    sl = n % 2
                    wait_free("gpsimd", ("dwnw", sl))
                    P.op("gpsimd", lambda e, e_=e_, n=n, sl=sl: e.dma_start(
                        out=dwnw[:, sl, :, :], in_=DWN[e_, n]), inc=sDW[sl], n=16)
                    P.wait("tensor", sDW[sl])
                    for mh in range(2):
                        ms = range(4 * mh, 4 * mh + 4)
                        for m in ms:
                            wait_free("tensor", ("dbank", m))
                        for k in range(11):
                            for m in ms:
                                P.op("tensor", lambda e, sl=sl, k=k, m=m: e.matmul(
                                    bank8(m), he[:, k, m * 128:(m + 1) * 128], dwnw[:, sl, k, :],
                                    start=(k == 0), stop=(k == 10)),
                                    inc=(sT if (k == 10 and m == 4 * mh + 3) else None))
                        v_mm = sT.v
                        if mh == 1:
                            free_set(("dwnw", sl), (sT, v_mm))
                        P.wait("vector", sT, v_mm)
                        for m in ms:
                            cap = combT[:, m, e_:e_ + 1]
                            asl = acc[:, m, n * 512:(n + 1) * 512]
                            if e_ == 0:
                                wait_free("vector", ("accm", m))
                                v_up = P.op("vector", lambda e, m=m, cap=cap, asl=asl: e.tensor_scalar(
                                    asl, bank8(m), cap, None, ALU.mult), inc=sV)
                            else:
                                v_up = P.op("vector", lambda e, m=m, cap=cap, asl=asl: e.scalar_tensor_tensor(
                                    asl, bank8(m), cap, asl, ALU.mult, ALU.add), inc=sV)
                            free_set(("dbank", m), (sV, v_up))
                for hg in range(11):
                    free_set(("he", hg), (sT, sT.v))
        def moe_combine_units(hh):
            units = []
            v_acc_done = sV.v
            for m in range(8):
                units.append(lambda m=m, hh=hh, v_acc_done=v_acc_done: _combine_m(hh, m, v_acc_done))
            return units

        def _combine_m(hh, m, v_acc_done):
            if True:
                P.wait("scalar", sV, v_acc_done)
                wait_free("scalar", ("gbuf",))
                v_g = P.op("scalar", lambda e, m=m: e.activation(gbuf[:], acc[:, m, :], AF.Silu), inc=sS)
                free_set(("accm", m), (sS, v_g))
                rows = hh * HALF + m * 128
                wait_free("sync", ("outb", m % 2))
                P.op("sync", lambda e, rows=rows, m=m: e.dma_start(
                    out=outb[:, m % 2, :], in_=X[rows:rows + 128, :]), inc=sXL[m % 2], n=16)
                wait_free("sync", ("sob", m % 2))
                P.wait("sync", sSm[0])
                P.wait("sync", sSm[1])
                P.op("sync", lambda e, hh=hh, m=m: e.dma_start(
                    out=sob[:, m % 2, :], in_=SPS[hh * 8 + m]), inc=sSB[m % 2], n=16)
                P.wait("vector", sXL[m % 2])
                P.wait("vector", sSB[m % 2])
                P.wait("vector", sS, v_g)
                wait_free("vector", ("sobf",))
                v_sf = P.op("vector", lambda e, m=m: e.tensor_copy(sobf[:], sob[:, m % 2, :]), inc=sV)
                free_set(("sob", m % 2), (sV, v_sf))
                P.wait("vector", sV, v_sf)
                v_o1 = P.op("vector", lambda e, m=m: e.tensor_tensor(
                    gbuf[:], sobf[:], gbuf[:], ALU.mult), inc=sV)
                free_set(("sobf",), (sV, v_o1))
                P.wait("vector", sV, v_o1)
                v_o2 = P.op("vector", lambda e, m=m: e.tensor_tensor(
                    outb[:, m % 2, :], outb[:, m % 2, :], gbuf[:], ALU.add), inc=sV)
                free_set(("gbuf",), (sV, v_o2))
                P.wait("sync", sV, v_o2)
                v_st = P.op("sync", lambda e, rows=rows, m=m: e.dma_start(
                    out=OUT[rows:rows + 128, :], in_=outb[:, m % 2, :]), inc=sO[m % 2], n=16)
                free_set(("outb", m % 2), (sO[m % 2], v_st))

        wait_free("sync", ("xnT_h",))
        P.wait("sync", sSp)
        P.op("sync", lambda e: e.dma_start(out=xnT_h[:], in_=SPXNT[0]), inc=sXH, n=16)
        P.wait("tensor", sXH)
        router(0)
        moe_experts(0)
        wait_free("sync", ("xnT_h",))
        P.op("sync", lambda e: e.dma_start(out=xnT_h[:], in_=SPXNT[1]), inc=sXH, n=16)
        P.wait("tensor", sXH)
        P.wait("vector", sV)
        router(1)
        moe_experts(1, interleave=moe_combine_units(0))
        for u in moe_combine_units(1):
            u()

        P.wait("sync", sO[0])
        P.wait("sync", sO[1])
        P.wait("sync", sSp)
        P.wait("sync", sSm[0])
        P.wait("sync", sSm[1])
        if dbg:
            P.wait("sync", sOut)

        block = es.enter_context(nc.Block())
        for eng_name in Plan.ENGS:
            fns = P.streams[eng_name]
            getattr(block, eng_name)(lambda e, fns=fns: [fn(e) for fn in fns])

    return nc



_CACHE = {}


def _packs(inp):
    import ml_dtypes
    f32 = np.float32
    Wq = np.asarray(inp["L0_attn_Wq"], f32)
    Wk = np.asarray(inp["L0_attn_Wk"], f32)
    Wv = np.asarray(inp["L0_attn_Wv"], f32)
    Wo = np.asarray(inp["L0_attn_Wo"], f32)
    bq = np.asarray(inp["L0_attn_bq"], f32)
    bk = np.asarray(inp["L0_attn_bk"], f32)
    bv = np.asarray(inp["L0_attn_bv"], f32)
    gup = np.asarray(inp["L0_ffn_gate_up_proj"], f32)
    dwn = np.asarray(inp["L0_ffn_down_proj"], f32)
    rtr = np.asarray(inp["L0_ffn_router_w"], f32)

    WQp = np.empty((16, 128, 16, 128), f32)
    BQp = np.empty((16, 128), f32)
    WqT = Wq.reshape(2048, 16, 128).transpose(1, 2, 0)
    for t in range(16):
        A = 8 * (t // 4) + (t % 4)
        feats = np.concatenate([A * 64 + np.arange(64), (A + 4) * 64 + np.arange(64)])
        WQp[t] = WqT[:, :, feats].transpose(1, 0, 2)
        BQp[t] = bq[feats]
    WKp = np.ascontiguousarray(Wk.reshape(4, 128, 16, 128).transpose(0, 3, 2, 1))
    WVp = np.ascontiguousarray(Wv.reshape(4, 128, 16, 128).transpose(0, 3, 2, 1))
    BKp = np.ascontiguousarray(bk.reshape(4, 128))
    BVp = np.ascontiguousarray(bv.reshape(4, 128))
    WOp = np.ascontiguousarray(Wo.T.reshape(16, 128, 2048))

    gate = gup[:, :NH, :].reshape(NE, 11, 128, 16, 128)
    up = gup[:, NH:, :].reshape(NE, 11, 128, 16, 128)
    GUPp = np.ascontiguousarray(
        np.concatenate([gate.transpose(0, 1, 4, 3, 2), up.transpose(0, 1, 4, 3, 2)], axis=4))
    DWNp = np.ascontiguousarray(
        dwn.transpose(0, 2, 1).reshape(NE, 11, 128, 4, 512).transpose(0, 3, 2, 1, 4))
    RTRp = np.ascontiguousarray(rtr.T.reshape(16, 128, 8).transpose(1, 0, 2))
    EYEp = np.eye(128, dtype=f32)
    SELp = np.zeros((8, 32), f32)
    SELp[np.arange(32) // 4, np.arange(32)] = 1.0

    i_ = np.arange(512)[None, :]
    p_ = np.arange(128)[:, None]
    ebls = []
    for f in (0, 1):
        tbl = np.zeros((NG, 128, NEBL, 512), np.float32)
        for ti in range(NEBL):
            if ti < len(OWN_SIGMAS):
                sig, wrap = OWN_SIGMAS[ti], 0
            else:
                sig, wrap = OTHER_SIGMAS[ti - len(OWN_SIGMAS)], 4096 * f
            dist = np.abs(i_ - p_ + sig + wrap).astype(np.float32)
            for g in range(NG):
                tbl[g, :, ti, :] = np.exp(-SLOPES[g] * dist)
        ebls.append(tbl)

    shared = dict(WQ=WQp, WK=WKp, WV=WVp, WO=WOp, BQ=BQp, BK=BKp, BV=BVp,
                  GUP=GUPp, DWN=DWNp, RTR=RTRp, EYE=EYEp, SEL=SELp)
    return shared, ebls


def make_in_maps(inp):
    x = np.asarray(inp["L0_attn_hidden_states"], np.float32)
    shared, ebls = _packs(inp)
    in_maps = []
    for c in range(8):
        b, h = c // 2, c % 2
        Xc = x[b] if h == 0 else np.ascontiguousarray(np.roll(x[b], -OWN, axis=0))
        m = dict(shared)
        m["X"] = Xc
        m["EBL"] = ebls[h]
        in_maps.append(m)
    return in_maps


def kernel(**inp):
    if "nc" not in _CACHE:
        _CACHE["nc"] = build_program(dbg=False)
    nc = _CACHE["nc"]
    in_maps = make_in_maps(inp)
    res = run_bass_kernel_spmd(nc, in_maps, core_ids=list(range(8)))
    out = np.empty((NB, S, D), np.float32)
    for c in range(8):
        b, h = c // 2, c % 2
        out[b, h * OWN:(h + 1) * OWN] = res.results[c]["OUT"]
    return out
